# version 04 (final)
# baseline (speedup 1.0000x reference)
import math
import numpy as np
import ml_dtypes

import concourse.bass as bass
import concourse.mybir as mybir
from concourse.bass_utils import run_bass_kernel_spmd

F32 = mybir.dt.float32
BF16 = mybir.dt.bfloat16
I32 = mybir.dt.int32
AF = mybir.ActivationFunctionType
ALU = mybir.AluOpType
AX = mybir.AxisListType
BF = ml_dtypes.bfloat16
DS = bass.ds

B, N, D = 2, 8192, 1024
NH, HD = 16, 64
FH = 8
WIN = 512
HID = 2730
HIDP = 2816
CHUNK = 2048
HALO = 1280
L1EX = 640
RLEN = CHUNK + HALO
ULEN = CHUNK + RLEN
QK_SCALE = 8.0
MBIAS_SLACK = 72.0

MAGIC0 = 0x5A5A0000

ENGINES = ("tensor", "vector", "scalar", "gpsimd", "sync")


def core_cfg(c):
    pair = c // 2
    b = pair // 2
    parity = pair % 2
    role = c % 2
    if parity == 0:
        u_iv = [(0, 2048), (8192 - ULEN + 2048, 8192)]
        kvlen = 8192
        s0 = 0 if role == 0 else 6144
    else:
        u_iv = [(2048 - HALO, 6144)]
        kvlen = 6144
        s0 = 2048 if role == 0 else 4096
    s1 = s0 + CHUNK
    r0 = s0 - HALO
    heads = list(range(8 * role, 8 * role + 8))
    return dict(core=c, pair=pair, batch=b, parity=parity, role=role,
                u_iv=u_iv, kvlen=kvlen, s0=s0, s1=s1, r0=r0, heads=heads)


def u_tokens(parity):
    cfg_iv = [(0, 2048), (8192 - ULEN + 2048, 8192)] if parity == 0 else \
             [(2048 - HALO, 6144)]
    starts = []
    for a, bnd in cfg_iv:
        for t in range(a, bnd, 128):
            starts.append(t)
    assert len(starts) == ULEN // 128
    return starts


def u_pos_of(parity, g):
    if parity == 0:
        if g < 2048:
            return g
        base = 8192 - ULEN + 2048
        assert g >= base
        return 2048 + (g - base)
    return g - (2048 - HALO)


SEM_ROT = 60000


class Tok:
    __slots__ = ("sem", "val")
    def __init__(self, sem, val):
        self.sem = sem
        self.val = val


class Sched:
    def __init__(self):
        self.ops = {e: [] for e in ENGINES}
        self.streams = {}
        self.floors = {e: {} for e in ENGINES}
        self._floor_stack = []
        self._in_switch = False

    def stream(self, name, engine, inc=1, nsem=2):
        if name not in self.streams:
            self.streams[name] = dict(name=name, engine=engine, inc=inc,
                                      nsem=nsem, idx=0, ctr=0)
        return name

    def sem_names(self):
        out = []
        for st in self.streams.values():
            for i in range(st["nsem"]):
                out.append(f"{st['name']}.{i}")
        return out

    def _cur(self, st):
        return f"{st['name']}.{st['idx'] % st['nsem']}"

    def emit(self, engine, fn, deps=(), stream=None):
        waits = []
        fl = self.floors[engine]
        for t in deps:
            if t is None:
                continue
            if fl.get(t.sem, 0) >= t.val:
                continue
            fl[t.sem] = t.val
            waits.append((t.sem, t.val))
        tok = None
        inc_sem = None
        if stream is not None:
            st = self.streams[stream]
            if st["ctr"] + st["inc"] > SEM_ROT:
                assert not self._in_switch, "stream rotation inside Switch"
                st["idx"] += 1
                st["ctr"] = 0
            st["ctr"] += st["inc"]
            inc_sem = self._cur(st)
            tok = Tok(inc_sem, st["ctr"])
        self.ops[engine].append((waits, fn, inc_sem,
                                 self.streams[stream]["inc"] if stream else 0))
        return tok

    def last_tok(self, stream):
        st = self.streams[stream]
        return Tok(self._cur(st), st["ctr"])

    def switch_begin(self, idx_fn, n):
        self._floor_stack.append((
            {e: dict(self.floors[e]) for e in ENGINES},
            {k: (v["idx"], v["ctr"]) for k, v in self.streams.items()},
        ))
        self._in_switch = True
        for e in ENGINES:
            self.ops[e].append(("SWITCH_BEGIN", idx_fn, n, None))

    def switch_case(self):
        fl, stc = self._floor_stack[-1]
        for e in ENGINES:
            self.floors[e] = dict(fl[e])
            self.ops[e].append(("SWITCH_CASE", None, None, None))
        for k, (i, c) in stc.items():
            self.streams[k]["idx"] = i
            self.streams[k]["ctr"] = c

    def switch_end(self):
        fl, stc = self._floor_stack.pop()
        self._in_switch = False
        for e in ENGINES:
            self.floors[e] = dict(fl[e])
            self.ops[e].append(("SWITCH_END", None, None, None))
        for k, (i, c) in stc.items():
            self.streams[k]["idx"] = i
            self.streams[k]["ctr"] = c

    def manual_advance(self, stream, n=1):
        st = self.streams[stream]
        st["ctr"] += st["inc"] * n
        assert st["ctr"] <= SEM_ROT
        return Tok(self._cur(st), st["ctr"])

    def cut(self):
        ops = self.ops
        self.ops = {e: [] for e in ENGINES}
        return ops

    @staticmethod
    def materialize(ops_for_engine, eng, sems):
        sw_gen = None
        for item in ops_for_engine:
            if item[0] == "SWITCH_BEGIN":
                _, idx_fn, n, _ = item
                idx = idx_fn(eng, sems)
                sw_gen = eng.Switch(idx, n)
                next(sw_gen)
            elif item[0] == "SWITCH_CASE":
                next(sw_gen)
            elif item[0] == "SWITCH_END":
                try:
                    next(sw_gen)
                except StopIteration:
                    pass
                sw_gen = None
            else:
                waits, fn, inc_sem, inc_amt = item
                for (s, v) in waits:
                    eng.wait_ge(sems[s], v)
                ins = fn(eng, sems)
                if inc_sem is not None:
                    if ins is None:
                        ins = eng.nop()
                    ins.then_inc(sems[inc_sem], inc_amt)


def _pad_tokens(x, lo, hi, n):
    lo_pad = max(0, -lo)
    hi_pad = max(0, hi - n)
    core = x[max(lo, 0):min(hi, n)]
    if lo_pad or hi_pad:
        core = np.concatenate(
            [np.zeros((lo_pad,) + x.shape[1:], x.dtype), core,
             np.zeros((hi_pad,) + x.shape[1:], x.dtype)], axis=0)
    return np.ascontiguousarray(core)


def build_host_inputs(inputs):
    x0 = np.asarray(inputs["L0_attn_inputs"])
    Wqkv = np.asarray(inputs["L0_attn_Wqkv_w"])
    bqkv = np.asarray(inputs["L0_attn_Wqkv_b"])
    Wo = np.asarray(inputs["L0_attn_Wo_w"])
    bo = np.asarray(inputs["L0_attn_Wo_b"])
    ghost = np.asarray(inputs["L0_attn_ghost"])
    fw_qkv = np.asarray(inputs["L0_ffn_wqkv"])
    fqs = np.asarray(inputs["L0_ffn_q_scale"])
    fks = np.asarray(inputs["L0_ffn_k_scale"])
    fwo = np.asarray(inputs["L0_ffn_wo"])
    flg = np.asarray(inputs["L0_ffn_ln_g"])
    flb = np.asarray(inputs["L0_ffn_ln_b"])
    fw1 = np.asarray(inputs["L0_ffn_w1"])
    fb1 = np.asarray(inputs["L0_ffn_b1"])
    fw2 = np.asarray(inputs["L0_ffn_w2"])
    fb2 = np.asarray(inputs["L0_ffn_b2"])

    WqkvT = np.ascontiguousarray(Wqkv.T)

    w1p = np.zeros((2, 1024, 2 * HIDP), np.float32)
    w1p[:, :, :HID] = fw1[:, :, :HID]
    w1p[:, :, HIDP:HIDP + HID] = fw1[:, :, HID:]
    b1p = np.zeros((2, 2 * HIDP), np.float32)
    b1p[:, :HID] = fb1[:, :HID]
    b1p[:, HIDP:HIDP + HID] = fb1[:, HID:]
    w2p = np.zeros((2, HIDP, 1024), np.float32)
    w2p[:, :HID, :] = fw2

    half = np.arange(0, HD, 2, dtype=np.float64)
    inv = 1.0 / 10000.0 ** (half / HD)
    tt = np.arange(N, dtype=np.float64)
    fr = tt[:, None] * inv[None, :]
    fr2 = np.concatenate([fr, fr], 1)
    base = (half + 0.4 * HD) / (1.4 * HD)
    power = (tt - N // 2) / float(WIN // 2)
    sc = base[None, :] ** power[:, None]
    sc2 = np.concatenate([sc, sc], 1)
    cosq = (np.cos(fr2) * sc2).astype(np.float32)
    sinq = (np.sin(fr2) * sc2).astype(np.float32)
    cosk = (np.cos(fr2) / sc2).astype(np.float32)
    sink = (np.sin(fr2) / sc2).astype(np.float32)
    sgn = np.concatenate([-np.ones(32), np.ones(32)]).astype(np.float32)
    sinq_s = sinq * sgn
    sink_s = sink * sgn

    ident_bf = np.eye(128, dtype=np.float32).astype(BF)
    ident_f32 = np.eye(128, dtype=np.float32)

    amask = np.zeros((4, 128, 512), np.float32)
    for v in range(4):
        off = 128 * v
        q = np.arange(128)[:, None]
        k = np.arange(512)[None, :]
        amask[v] = np.where(k <= off + q, 0.0, -1e30)
    amask_pm = np.ascontiguousarray(amask.transpose(1, 0, 2))

    ind8 = np.zeros((128, 4, 8), np.float32)
    for ocp in range(4):
        ind8[0:64, ocp, 2 * ocp] = 1.0
        ind8[64:128, ocp, 2 * ocp + 1] = 1.0
    ind2f = np.zeros((2, 128), np.float32)
    ind2f[0, 0:64] = 1.0
    ind2f[1, 64:128] = 1.0

    in_maps = []
    for c in range(8):
        cfg = core_cfg(c)
        b = cfg["batch"]; parity = cfg["parity"]; role = cfg["role"]
        s0, s1, r0 = cfg["s0"], cfg["s1"], cfg["r0"]
        hs = cfg["heads"]
        kvlen = cfg["kvlen"]
        m = {}
        m["x0kvT"] = np.ascontiguousarray(x0[b].T)
        qcols = np.concatenate([np.arange(h * HD, (h + 1) * HD) for h in hs])
        kcols = 1024 + qcols
        vcols = 2048 + qcols
        m["wkv_a"] = np.ascontiguousarray(
            WqkvT[:, np.concatenate([kcols, vcols])])
        m["wq_a"] = np.ascontiguousarray(WqkvT[:, qcols])
        ball = bqkv[np.concatenate([qcols, kcols, vcols])]
        m["bqkv_a"] = np.ascontiguousarray(ball.reshape(12, 128).T)
        m["bvrow"] = np.ascontiguousarray(ball[1024:1536]).reshape(1, 512)
        m["ghost_a"] = np.ascontiguousarray(ghost[hs]).reshape(1, 8)
        m["wo_a"] = np.ascontiguousarray(Wo.T)
        m["borow"] = bo.reshape(1, 1024)
        m["ind8"] = ind8.astype(BF)
        m["ind2f"] = ind2f
        m["ones1"] = np.ones((1, 128), np.float32)
        m["myslot"] = np.array([[role]], np.int32)
        m["peslot"] = np.array([[1 - role]], np.int32)
        m["magic"] = np.array([[MAGIC0 + k for k in range(1, 9)]], np.int32)
        m["a_base2"] = np.array([[HALO + (u_pos_of(parity, r0)
                                          if r0 >= 0 else -HALO)]], np.int32)
        m["x0r"] = _pad_tokens(x0[b], r0, s1, N)
        m["fwqkv"] = np.ascontiguousarray(fw_qkv)
        m["fqs"] = fqs.reshape(2, 1, 64)
        m["fks"] = fks.reshape(2, 1, 64)
        m["fwo"] = np.ascontiguousarray(fwo)
        m["flg"] = np.ascontiguousarray(
            flg.reshape(2, 8, 128).transpose(0, 2, 1))
        m["flb"] = np.ascontiguousarray(
            flb.reshape(2, 8, 128).transpose(0, 2, 1))
        m["fw1"] = np.ascontiguousarray(w1p)
        m["fb1"] = np.ascontiguousarray(
            b1p.reshape(2, 44, 128).transpose(0, 2, 1))
        m["fw2"] = np.ascontiguousarray(w2p)
        m["fb2row"] = np.ascontiguousarray(fb2.reshape(2, 1, 1024))
        m["onescol"] = np.ones((128, 1), np.float32)
        m["rot_cq"] = _pad_tokens(cosq, r0, s1, N)
        m["rot_sq"] = _pad_tokens(sinq_s, r0, s1, N)
        m["rot_ck"] = _pad_tokens(cosk, r0, s1, N)
        m["rot_sk"] = _pad_tokens(sink_s, r0, s1, N)
        m["ident_bf"] = ident_bf
        m["ident_f32"] = ident_f32
        m["amask"] = amask_pm
        m["parity"] = np.array([[parity]], np.int32)
        m["role"] = np.array([[role]], np.int32)
        m["halo_keep"] = np.full((128, 1), 0.0 if s0 == 0 else 1.0, np.float32)
        m.update(build_stageb_masks(s0))
        in_maps.append(m)
    return in_maps


def build_stageb_masks(s0):
    out = {}
    for tag, nq, qstart in (("l1", (768 + CHUNK) // 128, s0 - 768),
                            ("l2", (128 + CHUNK) // 128, s0 - 128)):
        mk = np.zeros((nq, 128, 5 * 128), np.float32)
        for t in range(nq):
            qbase = qstart + t * 128
            q = qbase + np.arange(128)[:, None]
            k = (qbase - 512) + np.arange(5 * 128)[None, :]
            valid = (k <= q) & (q - k <= 512) & (k >= 0)
            fake = q < 0
            self_only = (k == q)
            valid = np.where(fake, self_only, valid)
            mk[t] = np.where(valid, 0.0, -1e30)
        out[f"mask_{tag}"] = np.ascontiguousarray(mk.astype(BF))
    return out


def _qgroups(parity):
    ivs = [(0, 2048), (8192 - ULEN + 2048, 8192)] if parity == 0 else \
          [(2048 - HALO, 6144)]
    out = []
    for a, bnd in ivs:
        t = a
        while t < bnd:
            nxt = min(bnd, (t // 512 + 1) * 512)
            out.append((t, nxt - t))
            t = nxt
    return out


def _spans(nkb):
    out = []
    kb = 0
    while kb < nkb:
        if len(out) % 2 == 0:
            nb = min(2, nkb - kb)
            out.append((kb, nb, "A"))
        else:
            out.append((kb, 1, "B"))
            nb = 1
        kb += nb
    return out


def emit_stageA_body(sch, parity, H):
    nc = H["nc"]
    PS = H["PS"]
    KT, Vt, wkv_sb = H["KT"], H["Vt"], H["wkv_sb"]
    x0g = H["x0g"]
    bq_sb, bq8_sb = H["bq_sb"], H["bq8_sb"]
    sq_sb, sqd_sb, bvb_sb = H["sq_sb"], H["sqd_sb"], H["bvb_sb"]
    kn2m_sb, kpre_sb, spre_sb = H["kn2m_sb"], H["kpre_sb"], H["spre_sb"]
    n2s_sb, sds_sb = H["n2s_sb"], H["sds_sb"]
    Mst_sb, MT_sb, MTf_sb = H["Mst_sb"], H["MT_sb"], H["MTf_sb"]
    amask_sb, identf_sb = H["amask_sb"], H["identf_sb"]
    ind8_sb, ind2_sb, ones1_sb = H["ind8_sb"], H["ind2_sb"], H["ones1_sb"]
    gbc_sb = H["gbc_sb"]
    QTt, zA, zB, zTA, zTB = H["QTt"], H["zA"], H["zB"], H["zTA"], H["zTB"]
    sig_sb, zmx_sb = H["sig_sb"], H["zmx_sb"]
    sig2_sb, zmx2_sb, den_sb, r2_sb = (H["sig2_sb"], H["zmx2_sb"],
                                       H["den_sb"], H["r2_sb"])
    rT_sb, rb_sb, Ast_sb = H["rT_sb"], H["rb_sb"], H["Ast_sb"]
    QTst_sb = H["QTst_sb"]
    myslot_sb = H["myslot_sb"]
    x0kvT, wkv_a, wq_a, QTd, xchA = (H["x0kvT"], H["wkv_a"], H["wq_a"],
                                     H["QTd"], H["xchA"])
    AstD = H["AstD"]
    tc, t_bq8, t_bv, t_gb2 = H["tc"], H["t_bq8"], H["t_bv"], H["t_gb2"]

    p = parity
    QGP, QSY, QSC = f"qgp{p}", f"qsy{p}", f"qsc{p}"
    PE, DVE, ACT = f"pe{p}", f"dve{p}", f"act{p}"
    kvlen = 8192 if parity == 0 else 6144
    ngrp = kvlen // 512
    ustarts = u_tokens(parity)

    def E(engine, fn, deps=(), stream=None):
        return sch.emit(engine, fn, deps=deps, stream=stream)

    BK = 512

    t_w = E("gpsimd", lambda eng, sems:
            eng.dma_start(out=wkv_sb[:],
                          in_=wkv_a.rearrange("(c p) o -> p c o", p=128)),
            deps=[tc], stream=QGP)
    ld_tok = [None, None]
    cons_tok = [None, None]
    kevac = [None] * 4
    n2cons = None
    sq_war = None
    vcons = t_bv
    t_kmax_all = []
    for g in range(ngrp):
        gi = g % 2
        ld_tok[gi] = E("gpsimd", lambda eng, sems, g=g, gi=gi:
                       eng.dma_start(
                           out=x0g[gi][:],
                           in_=x0kvT.rearrange("(c p) n -> p c n", p=128)
                               [:, :, g * 512:(g + 1) * 512]),
                       deps=[cons_tok[gi]], stream=QGP)
        for oc in range(4):
            dep0 = [ld_tok[gi], t_w, kevac[oc]]
            for ic in range(8):
                E("tensor", lambda eng, sems, oc=oc, ic=ic, gi=gi:
                  eng.matmul(PS[:, oc * BK:(oc + 1) * BK],
                             wkv_sb[:, ic, oc * 128:(oc + 1) * 128],
                             x0g[gi][:, ic, :],
                             start=(ic == 0), stop=(ic == 7)),
                  deps=dep0 if ic == 0 else (), stream=PE)
            t_mm = sch.last_tok(PE)
            E("scalar", lambda eng, sems, oc=oc, g=g:
              eng.activation(KT[:, oc, g * 512:(g + 1) * 512],
                             PS[:, oc * BK:(oc + 1) * BK], AF.Identity,
                             bias=bq_sb[:, 4 + oc:5 + oc]),
              deps=[t_mm], stream=ACT)
            t_e2 = E("scalar", lambda eng, sems, oc=oc:
                     eng.activation(sq_sb[:], PS[:, oc * BK:(oc + 1) * BK],
                                    AF.Square, bias=bq_sb[:, 4 + oc:5 + oc]),
                     deps=[t_mm, sq_war], stream=ACT)
            kevac[oc] = t_e2
            sq_war = E("tensor", lambda eng, sems, oc=oc:
                       eng.matmul(PS[0:8, 5 * BK:6 * BK], ind8_sb[:, oc, :],
                                  sq_sb[:], start=(oc == 0), stop=(oc == 3)),
                       deps=[t_e2, n2cons], stream=PE)
        t_kn = sch.last_tok(PE)
        t_km = E("vector", lambda eng, sems, g=g:
                 eng.reduce_max(kn2m_sb[:, g:g + 1], PS[0:8, 5 * BK:6 * BK],
                                axis=AX.X),
                 deps=[t_kn], stream=DVE)
        n2cons = t_km
        t_kmax_all.append(t_km)
        for t in range(4):
            for ic in range(8):
                E("tensor", lambda eng, sems, t=t, ic=ic, gi=gi:
                  eng.matmul(PS[:, 4 * BK:5 * BK],
                             x0g[gi][:, ic, t * 128:(t + 1) * 128],
                             wkv_sb[:, ic, 512:1024],
                             start=(ic == 0), stop=(ic == 7)),
                  deps=[vcons, ld_tok[gi], t_w] if ic == 0 else (),
                  stream=PE)
            t_vm = sch.last_tok(PE)
            vcons = E("vector", lambda eng, sems, g=g, t=t:
                      eng.scalar_tensor_tensor(
                          Vt[:, g * 4 + t, :], PS[:, 4 * BK:5 * BK], 1.0,
                          bvb_sb[:], ALU.mult, ALU.add),
                      deps=[t_vm, t_bv], stream=DVE)
        cons_tok[gi] = sch.last_tok(PE)
    t_vfinal = vcons

    t_p = E("vector", lambda eng, sems:
            eng.tensor_copy(kpre_sb[:, 0:1], kn2m_sb[:, 0:1]),
            deps=[t_kmax_all[0]], stream=DVE)
    for g in range(1, ngrp):
        t_p = E("vector", lambda eng, sems, g=g:
                eng.tensor_tensor(kpre_sb[:, g:g + 1], kpre_sb[:, g - 1:g],
                                  kn2m_sb[:, g:g + 1], ALU.max),
                deps=[t_kmax_all[g]], stream=DVE)
    t_sp = E("scalar", lambda eng, sems:
             eng.activation(spre_sb[:], kpre_sb[:], AF.Sqrt),
             deps=[t_p], stream=ACT)

    t_wq = E("gpsimd", lambda eng, sems:
             eng.dma_start(out=wkv_sb[:, :, 0:512],
                           in_=wq_a.rearrange("(c p) o -> p c o", p=128)),
             deps=[cons_tok[0], cons_tok[1], sch.last_tok(PE)], stream=QGP)
    qgroups = _qgroups(parity)
    qevac = [None] * 4
    bank5_war = None
    n2s_war = None
    sq_war2 = None
    sqd_war = None
    qst_war = None
    sdbank_war = None
    mst_war = None
    mst_toks = []
    upos = 0
    for gidx, (gstart, glen) in enumerate(qgroups):
        gi = gidx % 2
        ldq = E("gpsimd", lambda eng, sems, gstart=gstart, glen=glen, gi=gi:
                eng.dma_start(
                    out=x0g[gi][:, :, 0:glen],
                    in_=x0kvT.rearrange("(c p) n -> p c n", p=128)
                        [:, :, gstart:gstart + glen]),
                deps=[cons_tok[gi]], stream=QGP)
        for oc in range(4):
            for ic in range(8):
                E("tensor", lambda eng, sems, oc=oc, ic=ic, gi=gi, glen=glen:
                  eng.matmul(PS[:, oc * BK:oc * BK + glen],
                             wkv_sb[:, ic, oc * 128:(oc + 1) * 128],
                             x0g[gi][:, ic, 0:glen],
                             start=(ic == 0), stop=(ic == 7)),
                  deps=[ldq, t_wq, qevac[oc]] if ic == 0 else (), stream=PE)
            t_mm = sch.last_tok(PE)
            tq1 = E("scalar", lambda eng, sems, oc=oc, glen=glen:
                    eng.activation(QTst_sb[:, oc, 0:glen],
                                   PS[:, oc * BK:oc * BK + glen], AF.Identity,
                                   bias=bq8_sb[:, oc:oc + 1], scale=0.125),
                    deps=[t_mm, t_bq8, qst_war], stream=ACT)
            tq2 = E("scalar", lambda eng, sems, oc=oc, glen=glen:
                    eng.activation(sq_sb[:, 0:glen],
                                   PS[:, oc * BK:oc * BK + glen],
                                   AF.Square, bias=bq8_sb[:, oc:oc + 1],
                                   scale=0.125),
                    deps=[t_mm, t_bq8, sq_war2], stream=ACT)
            qevac[oc] = tq2
            sq_war2 = E("tensor", lambda eng, sems, oc=oc, glen=glen:
                        eng.matmul(PS[0:8, 5 * BK:5 * BK + glen],
                                   ind8_sb[:, oc, :], sq_sb[:, 0:glen],
                                   start=(oc == 0), stop=(oc == 3)),
                        deps=[tq2, bank5_war], stream=PE)
            tpd = E("vector", lambda eng, sems, oc=oc, glen=glen,
                           gstart=gstart:
                    eng.tensor_tensor(sqd_sb[:, 0:glen],
                                      QTst_sb[:, oc, 0:glen],
                                      KT[:, oc, gstart:gstart + glen],
                                      ALU.mult),
                    deps=[tq1, sqd_war], stream=DVE)
            sqd_war = E("tensor", lambda eng, sems, oc=oc, glen=glen:
                        eng.matmul(PS[0:8, 6 * BK:6 * BK + glen],
                                   ind8_sb[:, oc, :], sqd_sb[:, 0:glen],
                                   start=(oc == 0), stop=(oc == 3)),
                        deps=[tpd, sdbank_war], stream=PE)
        t_q = sch.last_tok(PE)
        qst_war = E("gpsimd", lambda eng, sems, upos=upos, glen=glen:
                    eng.dma_start(out=QTd[:, :, upos:upos + glen],
                                  in_=QTst_sb[:, :, 0:glen]),
                    deps=[qevac[3], tq1], stream=QGP)
        t_c1 = E("vector", lambda eng, sems, glen=glen:
                 eng.tensor_copy(n2s_sb[:, 0:glen],
                                 PS[0:8, 5 * BK:5 * BK + glen]),
                 deps=[t_q, n2s_war], stream=DVE)
        bank5_war = t_c1
        t_c2 = E("vector", lambda eng, sems, glen=glen:
                 eng.tensor_copy(sds_sb[:, 0:glen],
                                 PS[0:8, 6 * BK:6 * BK + glen]),
                 deps=[t_q], stream=DVE)
        sdbank_war = t_c2
        t_qn = E("scalar", lambda eng, sems, glen=glen:
                 eng.activation(n2s_sb[:, 0:glen], n2s_sb[:, 0:glen], AF.Sqrt),
                 deps=[t_c1], stream=ACT)
        n2s_war = t_qn
        blk = gstart // 512
        t_m1 = E("vector", lambda eng, sems, glen=glen, blk=blk:
                 eng.tensor_scalar(n2s_sb[:, 0:glen], n2s_sb[:, 0:glen],
                                   spre_sb[:, blk:blk + 1], None, ALU.mult),
                 deps=[t_qn, t_sp], stream=DVE)
        t_m2 = E("vector", lambda eng, sems, glen=glen:
                 eng.tensor_scalar(n2s_sb[:, 0:glen], n2s_sb[:, 0:glen],
                                   -1.0, MBIAS_SLACK, ALU.mult, ALU.add),
                 deps=[t_m1], stream=DVE)
        t_m3 = E("vector", lambda eng, sems, glen=glen:
                 eng.tensor_scalar_mul(sds_sb[:, 0:glen], sds_sb[:, 0:glen],
                                       -1.0),
                 deps=[t_c2], stream=DVE)
        t_m4 = E("vector", lambda eng, sems, glen=glen, mst_war=mst_war:
                 eng.tensor_tensor(Mst_sb[0:8, 0:glen],
                                   n2s_sb[:, 0:glen], sds_sb[:, 0:glen],
                                   ALU.min),
                 deps=[t_m2, t_m3, mst_war], stream=DVE)
        n2s_war = t_m4
        mst_war = E("sync", lambda eng, sems, glen=glen, upos=upos:
                    eng.dma_start_transpose(
                        MT_sb[:, upos // 128:(upos + glen) // 128, :],
                        Mst_sb[:, 0:glen]),
                    deps=[t_m4], stream=QSY)
        mst_toks.append(mst_war)
        upos += glen

    t_mtf = E("vector", lambda eng, sems:
              eng.tensor_copy(MTf_sb[:].rearrange("p a b -> p (a b)"),
                              MT_sb[:].rearrange("p a b -> p (a b)")),
              deps=mst_toks, stream=DVE)

    SAo = [0 * BK, 2 * BK]
    SBo = [4 * BK, 5 * BK]
    Oo = 6 * BK
    AUXo = 7 * BK

    qt_ld = [None, None]
    qt_cons = [None, None]
    span_exp = {}
    span_tr = {}
    span_pv = {}
    span_zmx = {}
    sig_war = None
    zmx_war = None
    o_cons = None
    aux_cons = H["t_bv"]
    ast_cons = [None] * 4
    t_slot = t_mtf

    t_fence = E("tensor", lambda eng, sems: eng.nop(),
                deps=[qevac[0], qevac[1], qevac[2], qevac[3], t_mtf,
                      bank5_war, sdbank_war, sq_war2, sqd_war, qst_war],
                stream=PE)

    for qi in range(len(ustarts)):
        gtok = ustarts[qi]
        gi = qi % 2
        qt_ld[gi] = E("gpsimd", lambda eng, sems, qi=qi, gi=gi:
                      eng.dma_start(out=QTt[gi][:],
                                    in_=QTd[:, :, qi * 128:(qi + 1) * 128]),
                      deps=[qt_cons[gi],
                            sch.last_tok(QGP) if qi == 0 else None],
                      stream=QGP)
        nkb = (gtok + 128 + 511) // 512
        spans = _spans(nkb)
        ns = len(spans)
        voff = (gtok % 512) // 128
        t_store_dep = []
        for p4 in range(4):
            first_pv = True
            for si, (kb0, nb, kind) in enumerate(spans):
                So = SAo if kind == "A" else SBo
                zbuf = zA if kind == "A" else zB
                zTbuf = zTA if kind == "A" else zTB
                for j in range(nb):
                    last_blk = (kb0 + j == nkb - 1)
                    for h01 in range(2):
                        E("tensor",
                          lambda eng, sems, j=j, h01=h01, kb0=kb0, So=So,
                                 gi=gi, p4=p4:
                          eng.matmul(
                              PS[:, So[h01] + j * BK:So[h01] + (j + 1) * BK],
                              QTt[gi][h01 * 64:(h01 + 1) * 64, p4, :],
                              KT[h01 * 64:(h01 + 1) * 64, p4,
                                 (kb0 + j) * 512:(kb0 + j + 1) * 512],
                              start=True, stop=True,
                              tile_position=(h01 * 64, 0)),
                          deps=[qt_ld[gi], span_exp.get((kind, h01)),
                                t_slot if qi == 0 else None,
                                H["t_zero"] if qi == 0 else None],
                          stream=PE)
                t_smm = sch.last_tok(PE)
                t_msk = None
                if kb0 + nb == nkb:
                    for h01 in range(2):
                        t_msk = E("vector",
                                  lambda eng, sems, h01=h01, nb=nb, So=So,
                                         voff=voff:
                                  eng.scalar_tensor_tensor(
                                      PS[:, So[h01] + (nb - 1) * BK:
                                             So[h01] + nb * BK],
                                      PS[:, So[h01] + (nb - 1) * BK:
                                             So[h01] + nb * BK],
                                      1.0, amask_sb[:, voff, :],
                                      ALU.mult, ALU.add),
                                  deps=[t_smm], stream=DVE)
                for h01 in range(2):
                    h = 2 * p4 + h01
                    t_e = E("scalar",
                            lambda eng, sems, h01=h01, h=h, nb=nb, So=So,
                                   kind=kind, qi=qi, si=si:
                            eng.activation(
                                (zA if kind == "A" else zB)[h01][:, 0:nb * 512],
                                PS[:, So[h01]:So[h01] + nb * BK], AF.Exp,
                                bias=MTf_sb[:, qi, h:h + 1],
                                accum_out=sig_sb[:, h01, si:si + 1]),
                            deps=[t_smm, t_msk, t_mtf,
                                  span_tr.get((kind, h01)),
                                  span_zmx.get((kind, h01)), sig_war],
                            stream=ACT)
                    span_exp[(kind, h01)] = t_e
                    t_zm = E("vector",
                             lambda eng, sems, h01=h01, nb=nb, kind=kind,
                                    si=si:
                             eng.reduce_max(zmx_sb[:, h01, si:si + 1],
                                            (zA if kind == "A" else zB)[h01]
                                            [:, 0:nb * 512], axis=AX.X),
                             deps=[t_e, zmx_war], stream=DVE)
                    span_zmx[(kind, h01)] = t_zm
                    t_tr = E("sync",
                             lambda eng, sems, h01=h01, nb=nb, kind=kind:
                             eng.dma_start_transpose(
                                 (zTA if kind == "A" else zTB)[h01]
                                 [:, 0:nb * 4, :],
                                 (zA if kind == "A" else zB)[h01]
                                 [:, 0:nb * 512]),
                             deps=[t_e, span_pv.get((kind, h01))],
                             stream=QSY)
                    span_tr[(kind, h01)] = t_tr
                for j in range(nb * 4):
                    kc = kb0 * 4 + j
                    for h01 in range(2):
                        h = 2 * p4 + h01
                        tpv = E("tensor",
                                lambda eng, sems, j=j, kc=kc, h=h, h01=h01,
                                       kind=kind, fp=first_pv, si=si, ns=ns,
                                       nb=nb:
                                eng.matmul(
                                    PS[h01 * 64:(h01 + 1) * 64, Oo:Oo + 128],
                                    Vt[:, kc, h * 64:(h + 1) * 64],
                                    (zTA if kind == "A" else zTB)[h01][:, j, :],
                                    start=(fp and j == 0),
                                    stop=(si == ns - 1 and j == nb * 4 - 1),
                                    tile_position=(0, h01 * 64)),
                                deps=[span_tr[(kind, h01)],
                                      o_cons if (first_pv and j == 0) else None,
                                      t_vfinal if (first_pv and j == 0) else None],
                                stream=PE)
                        span_pv[(kind, h01)] = tpv
                first_pv = False
            t_pv_all = sch.last_tok(PE)
            t_s2 = E("vector", lambda eng, sems, ns=ns:
                     eng.reduce_sum(sig2_sb[:], sig_sb[:, :, 0:ns], axis=AX.X),
                     deps=[sch.last_tok(ACT)], stream=DVE)
            sig_war = t_s2
            t_z2 = E("vector", lambda eng, sems, ns=ns:
                     eng.reduce_max(zmx2_sb[:], zmx_sb[:, :, 0:ns], axis=AX.X),
                     deps=[sch.last_tok(DVE)], stream=DVE)
            zmx_war = t_z2
            t_den = []
            for h01 in range(2):
                h = 2 * p4 + h01
                t_den.append(E("vector", lambda eng, sems, h01=h01, h=h:
                               eng.scalar_tensor_tensor(
                                   den_sb[:, h01:h01 + 1],
                                   zmx2_sb[:, h01:h01 + 1],
                                   gbc_sb[:, h:h + 1],
                                   sig2_sb[:, h01:h01 + 1],
                                   ALU.mult, ALU.add),
                               deps=[t_s2, t_z2, t_gb2], stream=DVE))
            t_r = E("vector", lambda eng, sems:
                    eng.reciprocal(r2_sb[:], den_sb[:]),
                    deps=t_den, stream=DVE)
            t_rt = E("tensor", lambda eng, sems:
                     eng.transpose(PS[0:2, AUXo:AUXo + 128], r2_sb[:],
                                   identf_sb[:]),
                     deps=[t_r, aux_cons], stream=PE)
            t_rc = E("vector", lambda eng, sems:
                     eng.tensor_copy(rT_sb[:], PS[0:2, AUXo:AUXo + 128]),
                     deps=[t_rt], stream=DVE)
            t_rb = E("tensor", lambda eng, sems:
                     eng.matmul(PS[:, AUXo + 128:AUXo + 256], ind2_sb[:],
                                rT_sb[:], start=True, stop=True),
                     deps=[t_rc], stream=PE)
            t_rbc = E("vector", lambda eng, sems:
                      eng.tensor_copy(rb_sb[:], PS[:, AUXo + 128:AUXo + 256]),
                      deps=[t_rb], stream=DVE)
            aux_cons = t_rbc
            t_ast = E("vector", lambda eng, sems, p4=p4:
                      eng.tensor_tensor(Ast_sb[:, p4, :], PS[:, Oo:Oo + 128],
                                        rb_sb[:], ALU.mult),
                      deps=[t_pv_all, t_rbc, ast_cons[p4]], stream=DVE)
            o_cons = t_ast
            t_store_dep.append(t_ast)
        t_as = E("scalar", lambda eng, sems, qi=qi:
                 eng.dma_start(
                     out=AstD.rearrange("(c p) n -> p c n", p=128)
                         [:, :, HALO + qi * 128:HALO + (qi + 1) * 128],
                     in_=Ast_sb[:]),
                 deps=t_store_dep, stream=QSC)
        for p4 in range(4):
            ast_cons[p4] = t_as
        qt_cons[gi] = sch.last_tok(PE)

    ends = dict(tensor=[sch.last_tok(PE)], vector=[sch.last_tok(DVE)],
                scalar=[sch.last_tok(ACT), sch.last_tok(QSC)],
                gpsimd=[sch.last_tok(QGP)], sync=[sch.last_tok(QSY)])
    for ename, deps in ends.items():
        def bridge_fn(eng, sems):
            ins = eng.nop()
            ins.then_inc(sems["bridge.0"], 1)
            return None
        sch.emit(ename, bridge_fn, deps=deps, stream=None)


def emit_wo_x1(sch, H):
    nc = H["nc"]
    PS2 = H["PS2"]
    xres, At_sb, wo_sb, bob_sb = H["xres"], H["At_sb"], H["wo_sb"], H["bob_sb"]
    xstg, x1loD = H["xstg"], H["x1loD"]

    def xt(t):
        return xstg[:] if t < 4 else xres[:, t - 4, :]
    x0t_sb, hk_sb = H["x0t_sb"], H["hk_sb"]
    AtD, t_atd = H["AtD"], H["t_atd"]
    lnsc, sums_sb, xpre_sb = H["lnsc"], H["sums_sb"], H["xpre_sb"]
    xchA, x0r, x1d = H["xchA"], H["x0r"], H["x1d"]
    t_poll, t_bob, tc2 = H["t_poll"], H["t_bob"], H["tc2"]
    t_bridge = H["t_bridge"]

    BK = 512
    ntile = RLEN // 128
    a_cons = None
    x0_cons = [None, None]
    w_cons = [None, None]
    sums_war_q = None
    sums_war_sq = None
    xpre_war = None
    xstg_war = None

    for t in range(ntile):
        wb = t % 2
        sch.emit("scalar", lambda eng, sems, t=t:
                 eng.dma_start(
                     out=At_sb[:],
                     in_=AtD.rearrange("(c p) n -> p c n", p=128)
                         [:, :, t * 128:(t + 1) * 128]),
                 deps=[t_atd, a_cons, tc2], stream="qsc")
        t_a = sch.last_tok("qsc")
        t_x0 = sch.emit("gpsimd", lambda eng, sems, t=t, wb=wb:
                        eng.dma_start(out=x0t_sb[wb][:],
                                      in_=x0r[t * 128:(t + 1) * 128, :]),
                        deps=[x0_cons[wb]], stream="qgp")
        for ic in range(8):
            sch.emit("tensor",
                     lambda eng, sems, ic=ic, wb=wb:
                     eng.matmul(PS2[:, wb * 2 * BK + 0:wb * 2 * BK + BK],
                                At_sb[:, ic, :], wo_sb[:, ic, 0:512],
                                start=(ic == 0), stop=(ic == 7)),
                     deps=[t_a, w_cons[wb]] if ic == 0 else (), stream="pe")
            sch.emit("tensor",
                     lambda eng, sems, ic=ic, wb=wb:
                     eng.matmul(PS2[:, wb * 2 * BK + BK:wb * 2 * BK + 2 * BK],
                                At_sb[:, ic, :], wo_sb[:, ic, 512:1024],
                                start=(ic == 0), stop=(ic == 7)),
                     deps=(), stream="pe")
        t_wo = sch.last_tok("pe")
        a_cons = t_wo
        t_p1 = sch.emit("vector", lambda eng, sems, wb=wb:
                        eng.scalar_tensor_tensor(
                            xpre_sb[:], PS2[:, wb * 2 * BK:(wb * 2 + 2) * BK],
                            1.0, bob_sb[:], ALU.mult, ALU.add),
                        deps=[t_wo, t_bob, xpre_war], stream="dve")
        t_p2 = sch.emit("vector", lambda eng, sems, wb=wb:
                        eng.tensor_add(xpre_sb[:], xpre_sb[:], x0t_sb[wb][:]),
                        deps=[t_p1, t_x0], stream="dve")
        w_cons[wb] = t_p1
        x0_cons[wb] = t_p2
        t_s = sch.emit("vector", lambda eng, sems:
                       eng.reduce_sum(sums_sb[:, 0:1], xpre_sb[:], axis=AX.X),
                       deps=[t_p2], stream="dve")
        t_q = sch.emit("scalar", lambda eng, sems:
                       eng.activation(lnsc[:], xpre_sb[:], AF.Square,
                                      accum_out=sums_sb[:, 1:2]),
                       deps=[t_p2, sums_war_q], stream="act")
        t_mu = sch.emit("vector", lambda eng, sems:
                        eng.tensor_scalar_mul(sums_sb[:, 2:3], sums_sb[:, 0:1],
                                              1.0 / 1024.0),
                        deps=[t_s], stream="dve")
        t_m2 = sch.emit("vector", lambda eng, sems:
                        eng.tensor_tensor(sums_sb[:, 3:4], sums_sb[:, 2:3],
                                          sums_sb[:, 2:3], ALU.mult),
                        deps=[t_mu], stream="dve")
        t_var = sch.emit("vector", lambda eng, sems:
                         eng.tensor_scalar(sums_sb[:, 4:5], sums_sb[:, 1:2],
                                           1.0 / 1024.0, None, ALU.mult),
                         deps=[t_q], stream="dve")
        t_var2 = sch.emit("vector", lambda eng, sems:
                          eng.tensor_sub(sums_sb[:, 4:5], sums_sb[:, 4:5],
                                         sums_sb[:, 3:4]),
                          deps=[t_var, t_m2], stream="dve")
        t_ve = sch.emit("vector", lambda eng, sems:
                        eng.tensor_scalar(sums_sb[:, 5:6], sums_sb[:, 4:5],
                                          1e-5, None, ALU.add),
                        deps=[t_var2], stream="dve")
        t_sq = sch.emit("scalar", lambda eng, sems:
                        eng.activation(sums_sb[:, 6:7], sums_sb[:, 5:6],
                                       AF.Sqrt),
                        deps=[t_ve, sums_war_sq], stream="act")
        t_rs = sch.emit("vector", lambda eng, sems:
                        eng.reciprocal(sums_sb[:, 7:8], sums_sb[:, 6:7]),
                        deps=[t_sq], stream="dve")
        sums_war_q = t_var
        sums_war_sq = t_rs
        xpre_war = t_q
        t_x1 = sch.emit("vector", lambda eng, sems, t=t:
                        eng.tensor_scalar(xt(t), xpre_sb[:],
                                          sums_sb[:, 2:3], sums_sb[:, 7:8],
                                          ALU.subtract, ALU.mult),
                        deps=[t_rs] + ([xstg_war] if t < 4 else []),
                        stream="dve")
        if t < HALO // 128:
            t_x1 = sch.emit("vector", lambda eng, sems, t=t:
                            eng.tensor_scalar(xt(t), xt(t),
                                              hk_sb[:, 0:1], None, ALU.mult),
                            deps=[t_x1, tc2], stream="dve")
            if t < 4:
                xstg_war = sch.emit(
                    "gpsimd", lambda eng, sems, t=t:
                    eng.dma_start(out=x1loD[t * 128:(t + 1) * 128, :],
                                  in_=xstg[:]),
                    deps=[t_x1], stream="qgp")
        else:
            sch.emit("gpsimd", lambda eng, sems, t=t:
                     eng.dma_start(
                         out=x1d[(t - HALO // 128) * 128:
                                 (t - HALO // 128 + 1) * 128, :],
                         in_=xt(t)),
                     deps=[t_x1], stream="qgp")


def _ln_tokmaj(sch, H, src_tile, dst_writer, deps, war):
    lnsc, sums_sb = H["lnsc"], H["sums_sb"]
    t_s = sch.emit("vector", lambda eng, sems:
                   eng.reduce_sum(sums_sb[:, 0:1], src_tile, axis=AX.X),
                   deps=deps, stream="dve")
    t_q = sch.emit("scalar", lambda eng, sems:
                   eng.activation(lnsc[:], src_tile, AF.Square,
                                  accum_out=sums_sb[:, 1:2]),
                   deps=deps + [war[0]], stream="act")
    t_mu = sch.emit("vector", lambda eng, sems:
                    eng.tensor_scalar_mul(sums_sb[:, 2:3], sums_sb[:, 0:1],
                                          1.0 / 1024.0),
                    deps=[t_s], stream="dve")
    t_m2 = sch.emit("vector", lambda eng, sems:
                    eng.tensor_tensor(sums_sb[:, 3:4], sums_sb[:, 2:3],
                                      sums_sb[:, 2:3], ALU.mult),
                    deps=[t_mu], stream="dve")
    t_v1 = sch.emit("vector", lambda eng, sems:
                    eng.tensor_scalar(sums_sb[:, 4:5], sums_sb[:, 1:2],
                                      1.0 / 1024.0, None, ALU.mult),
                    deps=[t_q], stream="dve")
    t_v2 = sch.emit("vector", lambda eng, sems:
                    eng.tensor_sub(sums_sb[:, 4:5], sums_sb[:, 4:5],
                                   sums_sb[:, 3:4]),
                    deps=[t_v1, t_m2], stream="dve")
    t_v3 = sch.emit("vector", lambda eng, sems:
                    eng.tensor_scalar(sums_sb[:, 5:6], sums_sb[:, 4:5],
                                      1e-5, None, ALU.add),
                    deps=[t_v2], stream="dve")
    t_sq = sch.emit("scalar", lambda eng, sems:
                    eng.activation(sums_sb[:, 6:7], sums_sb[:, 5:6], AF.Sqrt),
                    deps=[t_v3, war[1]], stream="act")
    t_rs = sch.emit("vector", lambda eng, sems:
                    eng.reciprocal(sums_sb[:, 7:8], sums_sb[:, 6:7]),
                    deps=[t_sq], stream="dve")
    t_o = dst_writer(t_rs)
    return t_o, (t_v1, t_rs)


def emit_stageB(sch, H):
    nc = H["nc"]
    PS2 = H["PS2"]
    xres = H["xres"]
    out_t, x1d = H["out_t"], H["x1d"]
    lnsc, sums_sb = H["lnsc"], H["sums_sb"]
    identb_sb = H["identb_sb"]
    ones1b_sb = H["ones1b_sb"]
    onescol_sb = H["onescol_sb"]
    hk_sb = H["hk_sb"]
    wc_sb = H["wc_sb"]
    fwqkv, fqs, fks, fwo = H["fwqkv_i"], H["fqs_i"], H["fks_i"], H["fwo_i"]
    flg, flb, fb1 = H["flg_i"], H["flb_i"], H["fb1_i"]
    fb2row = H["fb2row_i"]
    fw1_i, fw2_i = H["fw1_i"], H["fw2_i"]
    rots = [H["rot_cq_i"], H["rot_sq_i"], H["rot_ck_i"], H["rot_sk_i"]]
    masks_i = [H["mask_l1_i"], H["mask_l2_i"]]
    qtokD = H["qtokD"]
    kpD, vFD = H["kpD"], H["vFD"]
    w1bd, w2bd = H["w1bd"], H["w2bd"]
    x1loD = H["x1loD"]

    BK = 512
    E = sch.emit
    from contextlib import ExitStack

    def xt(t):
        assert t >= 4
        return xres[:, t - 4, :]

    def fence(toks):
        toks = [t for t in toks if t is not None]
        for en in ("tensor", "vector", "scalar", "gpsimd", "sync"):
            E(en, lambda eng, sems: eng.nop(), deps=toks, stream=None)

    def phase_fence():
        fence([sch.last_tok(s)
               for s in ("pe", "dve", "act", "qgp", "qsy", "qsc")])

    def emit_wcvt():
        toks = {}
        prev = [None, None]
        ci = 0
        for l in range(2):
            for (tag, src, dst, nch, cw) in (
                    ("w1", fw1_i[l].rearrange("(c p) o -> p c o", p=128),
                     w1bd[l].rearrange("(c p) o -> p c o", p=128), 8, 5632),
                    ("w2", fw2_i[l].rearrange("(c p) o -> p c o", p=128),
                     w2bd[l].rearrange("(c p) o -> p c o", p=128), 22, 1024)):
                nsl = cw // 1408 if cw % 1408 == 0 else None
                slices = ([(j * 1408, (j + 1) * 1408) for j in range(nsl)]
                          if nsl else [(0, cw)])
                for c in range(nch):
                    for (a, b) in slices:
                        buf = ci % 2
                        t_in = E("gpsimd",
                                 lambda eng, sems, src=src, c=c, a=a, b=b,
                                        buf=buf:
                                 eng.dma_start(out=wc_sb[buf][:, 0:b - a],
                                               in_=src[:, c, a:b]),
                                 deps=[prev[buf]], stream="wcv")
                        t_out = E("gpsimd",
                                  lambda eng, sems, dst=dst, c=c, a=a, b=b,
                                         buf=buf:
                                  eng.dma_start(out=dst[:, c, a:b],
                                                in_=wc_sb[buf][:, 0:b - a]),
                                  deps=[t_in], stream="wcv")
                        prev[buf] = t_out
                        ci += 1
                toks[(tag, l)] = sch.last_tok("wcv")
        return toks

    wcvt_toks = None
    layer_bar = None

    for layer in range(2):
        if layer == 0:
            kvt0, nkv = 0, 26
            at0, nat = 4, 22
            mt0, nmt = 5, 21
        else:
            kvt0, nkv = 5, 21
            at0, nat = 9, 17
            mt0, nmt = 10, 16
        mask_i = masks_i[layer]

        if layer == 0:
            fence(list(H["t_entry"]))
        else:
            fence(layer_bar)

        L_wo = ExitStack()
        fwo_sb = L_wo.enter_context(
            nc.sbuf_tensor(f"fwoS{layer}", [128, 4, 1024], BF16))
        t_wo = E("gpsimd", lambda eng, sems, layer=layer:
                 eng.dma_start(out=fwo_sb[:],
                               in_=fwo[layer].rearrange("(c p) o -> p c o",
                                                        p=128)),
                 deps=[], stream="qgp")

        L_o = ExitStack()
        oT = L_o.enter_context(
            nc.sbuf_tensor(f"oT{layer}", [128, 4, 22 * 128], BF16))

        L_kv = ExitStack()
        sbL = L_kv.enter_context
        kT = sbL(nc.sbuf_tensor(f"kT{layer}", [128, 4, 6, 128], BF16))
        vF = sbL(nc.sbuf_tensor(f"vF{layer}", [128, 6, 512], BF16))
        negM_sb = sbL(nc.sbuf_tensor(f"negM{layer}", [128, 1], F32))

        P1 = ExitStack()
        sb1 = P1.enter_context
        wq_f = sb1(nc.sbuf_tensor(f"wqf{layer}", [128, 8, 1536], BF16))
        hT = sb1(nc.sbuf_tensor(f"hTb{layer}", [128, 8, 256], BF16))
        xc = sb1(nc.sbuf_tensor(f"xcb{layer}", [128, 1024], BF16))
        qk = sb1(nc.sbuf_tensor(f"qkb{layer}", [128, 1024], F32))
        nrm = sb1(nc.sbuf_tensor(f"nrmb{layer}", [128, 16], F32))
        rb1 = sb1(nc.sbuf_tensor(f"rb1b{layer}", [128, 8, 64], F32))
        rb2 = sb1(nc.sbuf_tensor(f"rb2b{layer}", [128, 8, 64], F32))
        rb3 = sb1(nc.sbuf_tensor(f"rb3b{layer}", [128, 8, 64], F32))
        qp = sb1(nc.sbuf_tensor(f"qpb{layer}", [128, 512], BF16))
        kp = sb1(nc.sbuf_tensor(f"kpb{layer}", [128, 512], BF16))
        hTs = [sb1(nc.sbuf_tensor(f"hTs{layer}_{i}", [128, 8, 128], BF16))
               for i in range(2)]
        tbl = sb1(nc.sbuf_tensor(f"tblb{layer}", [128, 4, 64], F32))
        srow = sb1(nc.sbuf_tensor(f"srowb{layer}", [1, 128], F32))
        qs_bc = sb1(nc.sbuf_tensor(f"qsbcb{layer}", [128, 2, 64], F32))
        vstg = [sb1(nc.sbuf_tensor(f"vstg{layer}_{i}", [128, 512], BF16))
                for i in range(2)]

        tpv = PS2[:, 1536:2048].bitcast(BF16)

        t_w = E("gpsimd", lambda eng, sems, layer=layer:
                eng.dma_start(out=wq_f[:],
                              in_=fwqkv[layer].rearrange("(c p) o -> p c o",
                                                         p=128)),
                deps=[], stream="qgp")
        t_qs = E("scalar", lambda eng, sems, layer=layer:
                 eng.dma_start(out=srow[0:1, 0:64], in_=fqs[layer]),
                 deps=[], stream="qsc")
        t_ks = E("scalar", lambda eng, sems, layer=layer:
                 eng.dma_start(out=srow[0:1, 64:128], in_=fks[layer]),
                 deps=[], stream="qsc")
        t_qb = E("tensor", lambda eng, sems:
                 eng.matmul(PS2[:, 3584:3712], ones1b_sb[:], srow[:],
                            start=True, stop=True),
                 deps=[t_qs, t_ks], stream="pe")
        t_qb2 = E("vector", lambda eng, sems:
                  eng.tensor_copy(qs_bc[:].rearrange("p a b -> p (a b)"),
                                  PS2[:, 3584:3712]),
                  deps=[t_qb], stream="dve")
        t_m1 = E("vector", lambda eng, sems:
                 eng.reduce_max(srow[0:1, 0:1], srow[0:1, 0:64], axis=AX.X,
                                apply_absolute_value=True),
                 deps=[t_qb], stream="dve")
        t_m2 = E("vector", lambda eng, sems:
                 eng.reduce_max(srow[0:1, 1:2], srow[0:1, 64:128], axis=AX.X,
                                apply_absolute_value=True),
                 deps=[t_qb], stream="dve")
        t_m3 = E("vector", lambda eng, sems:
                 eng.tensor_tensor(srow[0:1, 0:1], srow[0:1, 0:1],
                                   srow[0:1, 1:2], ALU.mult),
                 deps=[t_m1, t_m2], stream="dve")
        t_m4 = E("vector", lambda eng, sems:
                 eng.tensor_scalar_mul(srow[0:1, 0:1], srow[0:1, 0:1], -8.0),
                 deps=[t_m3], stream="dve")
        t_m5 = E("tensor", lambda eng, sems:
                 eng.matmul(PS2[:, 3712:3713], ones1b_sb[:], srow[0:1, 0:1],
                            start=True, stop=True),
                 deps=[t_m4, t_qb2], stream="pe")
        t_m6 = E("vector", lambda eng, sems:
                 eng.tensor_copy(negM_sb[:], PS2[:, 3712:3713]),
                 deps=[t_m5], stream="dve")

        t_hz = E("vector", lambda eng, sems:
                 eng.memset(hT[:], 0.0), deps=[], stream="dve")

        qkv_war = None
        tp_war = None
        hts_war = [None, None]
        qk_act = None
        w = {}
        vcons = None
        k_st = {}
        v_st = {}
        q_st = {}
        for ti in range(nkv):
            t = kvt0 + ti
            half = t % 2
            if t < 4:
                t_c = E("gpsimd", lambda eng, sems, t=t:
                        eng.dma_start(out=xc[:],
                                      in_=x1loD[t * 128:(t + 1) * 128, :]),
                        deps=[w.get("xc")], stream="qgp")
            else:
                t_c = E("vector", lambda eng, sems, t=t:
                        eng.tensor_copy(xc[:], xt(t)),
                        deps=[w.get("xc")], stream="dve")
            hb = ti % 2
            t_tr = E("sync", lambda eng, sems, hb=hb:
                     eng.dma_start_transpose(hTs[hb][:], xc[:]),
                     deps=[t_c, hts_war[hb]], stream="qsy")
            t_e1 = E("vector", lambda eng, sems, half=half, hb=hb:
                     eng.tensor_copy(
                         hT[:, 0:4, half * 128:(half + 1) * 128],
                         hTs[hb][:, 0:4, :]),
                     deps=[t_tr, t_hz, qkv_war], stream="dve")
            t_e2 = E("vector", lambda eng, sems, half=half, hb=hb:
                     eng.tensor_copy(
                         hT[:, 4:8, half * 128 + 1:half * 128 + 128],
                         hTs[hb][:, 4:8, 0:127]),
                     deps=[t_tr, qkv_war], stream="dve")
            t_e3 = E("vector", lambda eng, sems, half=half, hb=hb:
                     eng.tensor_copy(
                         hT[:, 4:8, (1 - half) * 128:(1 - half) * 128 + 1],
                         hTs[hb][:, 4:8, 127:128]),
                     deps=[t_tr], stream="dve")
            hts_war[hb] = t_e3
            w["xc"] = t_tr
            for ic in range(8):
                for sl in range(3):
                    E("tensor", lambda eng, sems, ic=ic, sl=sl, half=half:
                      eng.matmul(PS2[:, sl * BK:(sl + 1) * BK],
                                 hT[:, ic, half * 128:(half + 1) * 128],
                                 wq_f[:, ic, sl * 512:(sl + 1) * 512],
                                 start=(ic == 0), stop=(ic == 7)),
                      deps=[t_e1, t_e2, t_e3, t_w, qk_act, vcons]
                      if (ic == 0 and sl == 0) else (), stream="pe")
            t_qkv = sch.last_tok("pe")
            for tb in range(4):
                E("scalar", lambda eng, sems, tb=tb, t=t:
                  eng.dma_start(out=tbl[:, tb, :],
                                in_=rots[tb][t * 128:(t + 1) * 128, :]),
                  deps=[w.get("tbl")] if tb == 0 else (), stream="qsc")
            t_tb = sch.last_tok("qsc")
            t_eq = E("scalar", lambda eng, sems:
                     eng.activation(qk[:], PS2[:, 0:1024], AF.Identity),
                     deps=[t_qkv, w.get("qk_rd")], stream="act")
            qk_act = t_eq
            vb = ti % 2
            t_v = E("vector", lambda eng, sems, vb=vb:
                    eng.tensor_copy(vstg[vb][:], PS2[:, 1024:1536]),
                    deps=[t_qkv, w.get(("vstg", vb))], stream="dve")
            t_vst = E("scalar", lambda eng, sems, t=t, vb=vb:
                      eng.dma_start(out=vFD[t], in_=vstg[vb][:]),
                      deps=[t_v], stream="qsc")
            w[("vstg", vb)] = t_vst
            v_st[t] = t_vst
            vcons = t_v
            qkv_war = t_v
            t_sq = E("scalar", lambda eng, sems:
                     eng.activation(lnsc[:], qk[:], AF.Square),
                     deps=[t_eq, w.get("lnsc")], stream="act")
            t_rd = E("vector", lambda eng, sems:
                     eng.reduce_sum(nrm[:],
                                    lnsc[:].rearrange("p (s d) -> p s d",
                                                      d=64),
                                    axis=AX.X),
                     deps=[t_sq, w.get("nrm")], stream="dve")
            w["lnsc"] = t_rd
            t_sr = E("scalar", lambda eng, sems:
                     eng.activation(nrm[:], nrm[:], AF.Sqrt),
                     deps=[t_rd], stream="act")
            t_mx = E("vector", lambda eng, sems:
                     eng.tensor_scalar_max(nrm[:], nrm[:], 1e-12),
                     deps=[t_sr], stream="dve")
            t_rc = E("vector", lambda eng, sems:
                     eng.reciprocal(nrm[:], nrm[:]),
                     deps=[t_mx], stream="dve")
            w["nrm"] = t_rc
            side_toks = []
            for si2 in range(2):
                off = si2 * 512
                tb0 = si2 * 2
                pout = qp if si2 == 0 else kp
                t_s1 = E("vector", lambda eng, sems, off=off, si2=si2:
                         eng.tensor_tensor(
                             rb1[:],
                             qk[:, off:off + 512].rearrange(
                                 "p (s d) -> p s d", d=64),
                             qs_bc[:, si2:si2 + 1, :]
                             .broadcast_to([128, 8, 64]),
                             ALU.mult),
                         deps=[t_eq, t_qb2, w.get("rb1")], stream="dve")
                w["qk_rd"] = t_s1
                t_s2 = E("vector", lambda eng, sems, tb0=tb0:
                         eng.tensor_tensor(
                             rb2[:], rb1[:],
                             tbl[:, tb0:tb0 + 1, :]
                             .broadcast_to([128, 8, 64]),
                             ALU.mult),
                         deps=[t_s1, t_tb, w.get("rb2")], stream="dve")
                t_s3 = E("vector", lambda eng, sems, tb0=tb0:
                         eng.tensor_tensor(
                             rb3[:, :, 0:32], rb1[:, :, 32:64],
                             tbl[:, tb0 + 1:tb0 + 2, 0:32]
                             .broadcast_to([128, 8, 32]),
                             ALU.mult),
                         deps=[t_s1, w.get("rb3")], stream="dve")
                t_s4 = E("vector", lambda eng, sems, tb0=tb0:
                         eng.tensor_tensor(
                             rb3[:, :, 32:64], rb1[:, :, 0:32],
                             tbl[:, tb0 + 1:tb0 + 2, 32:64]
                             .broadcast_to([128, 8, 32]),
                             ALU.mult),
                         deps=[t_s1], stream="dve")
                w["rb1"] = t_s4
                t_s5 = E("vector", lambda eng, sems:
                         eng.tensor_add(rb2[:], rb2[:], rb3[:]),
                         deps=[t_s2, t_s3, t_s4], stream="dve")
                w["rb3"] = t_s5
                t_s6 = E("vector", lambda eng, sems, si2=si2, pout=pout:
                         eng.tensor_tensor(
                             pout[:].rearrange("p (s d) -> p s d", d=64),
                             rb2[:],
                             nrm[:, si2 * 8:(si2 + 1) * 8]
                             .unsqueeze(2).broadcast_to([128, 8, 64]),
                             ALU.mult),
                         deps=[t_s5, t_rc, w.get(("pq", si2))], stream="dve")
                w["rb2"] = t_s6
                side_toks.append(t_s6)
            t_kst = E("scalar", lambda eng, sems, t=t:
                      eng.dma_start(out=kpD[t], in_=kp[:]),
                      deps=[side_toks[1]], stream="qsc")
            w[("pq", 1)] = t_kst
            k_st[t] = t_kst
            t_qd = E("scalar", lambda eng, sems, t=t:
                     eng.dma_start(out=qtokD[t], in_=qp[:]),
                     deps=[side_toks[0]], stream="qsc")
            w[("pq", 0)] = t_qd
            q_st[t] = t_qd
            w["tbl"] = side_toks[1]
        if layer == 0:
            wcvt_toks = emit_wcvt()
        t_ph1 = dict(pe=sch.last_tok("pe"), dve=sch.last_tok("dve"),
                     act=sch.last_tok("act"), qgp=sch.last_tok("qgp"),
                     qsy=sch.last_tok("qsy"), qsc=sch.last_tok("qsc"))
        yield P1

        phase_fence()
        P2 = ExitStack()
        sb2 = P2.enter_context
        qT = [sb2(nc.sbuf_tensor(f"qT{layer}_{i}", [128, 4, 128], BF16))
              for i in range(2)]
        mk_sb = [sb2(nc.sbuf_tensor(f"mk{layer}_{i}", [128, 640], BF16))
                 for i in range(2)]
        zb = [sb2(nc.sbuf_tensor(f"zbb{layer}_{i}", [128, 640], BF16))
              for i in range(4)]
        zT = [sb2(nc.sbuf_tensor(f"zTb{layer}_{i}", [128, 5, 128], BF16))
              for i in range(4)]
        sig = sb2(nc.sbuf_tensor(f"sigb{layer}", [128, 8], F32))
        rr = sb2(nc.sbuf_tensor(f"rrb{layer}", [128, 8], F32))

        def s_base(st, h01):
            return st * 1536 + h01 * 768

        def s_ap(st, h01):
            b = s_base(st, h01)
            return PS2[:, b:b + 640]

        s_war = {}
        z_war = {}
        zT_war = {}
        qT_war = [None, None]
        mk_war = [None, None]
        o_war = {0: None, 1: None}
        sig_rd = None
        ring_war = {}
        kv_ld = {}

        def load_tile(t):
            sl = t % 6
            tk = E("sync", lambda eng, sems, t=t, sl=sl:
                   eng.dma_start_transpose(kT[:, :, sl, :], kpD[t]),
                   deps=[k_st[t], ring_war.get(sl)], stream="qsy")
            tv = E("scalar", lambda eng, sems, t=t, sl=sl:
                   eng.dma_start(out=vF[:, sl, :], in_=vFD[t]),
                   deps=[v_st[t], ring_war.get(sl)], stream="qsc")
            kv_ld[t] = (tk, tv)

        for tpre in range(at0 - 4, at0):
            load_tile(tpre)
        for qi in range(nat):
            qt = at0 + qi
            qb = qi % 2
            load_tile(qt)
            t_qT = E("sync", lambda eng, sems, qt=qt, qb=qb:
                     eng.dma_start_transpose(qT[qb][:], qtokD[qt]),
                     deps=[qT_war[qb], q_st[qt]], stream="qsy")
            t_mk = E("scalar", lambda eng, sems, qi=qi, qb=qb:
                     eng.dma_start(out=mk_sb[qb][:], in_=mask_i[qi]),
                     deps=[mk_war[qb]], stream="qsc")
            for p4 in range(4):
                st = (qi * 4 + p4) % 2
                for h01 in range(2):
                    for j in range(5):
                        E("tensor",
                          lambda eng, sems, h01=h01, p4=p4, qt=qt, qb=qb,
                                 st=st, j=j:
                          eng.matmul(
                              PS2[:, s_base(st, h01) + j * 128:
                                  s_base(st, h01) + (j + 1) * 128],
                              qT[qb][h01 * 64:(h01 + 1) * 64, p4, :],
                              kT[h01 * 64:(h01 + 1) * 64, p4,
                                 (qt - 4 + j) % 6, :],
                              start=True, stop=True,
                              tile_position=(h01 * 64, 0)),
                          deps=[t_qT, s_war.get((st, 0)), s_war.get((st, 1))]
                          + [kv_ld[qt - 4 + jj][0] for jj in range(5)]
                          if (h01 == 0 and j == 0) else (),
                          stream="pe")
                t_smm = sch.last_tok("pe")
                t_ms = []
                for h01 in range(2):
                    t_ms.append(E(
                        "vector",
                        lambda eng, sems, st=st, qb=qb, h01=h01:
                        eng.scalar_tensor_tensor(
                            s_ap(st, h01), s_ap(st, h01),
                            1.0, mk_sb[qb][:], ALU.mult, ALU.add),
                        deps=[t_smm, t_mk], stream="dve"))
                t_exp = {}
                for h01 in range(2):
                    h = 2 * p4 + h01
                    zslot = (p4 * 2 + h01) % 4
                    t_e = E("scalar",
                            lambda eng, sems, h01=h01, h=h, st=st,
                                   zslot=zslot:
                            eng.activation(
                                zb[zslot][:], s_ap(st, h01),
                                AF.Exp, bias=negM_sb[:, 0:1], scale=8.0,
                                accum_out=sig[:, h:h + 1]),
                            deps=[t_ms[h01], t_m6, z_war.get(zslot),
                                  sig_rd], stream="act")
                    s_war[(st, h01)] = t_e
                    t_exp[h01] = t_e
                t_r = E("vector", lambda eng, sems, p4=p4:
                        eng.reciprocal(rr[:, 2 * p4:2 * p4 + 2],
                                       sig[:, 2 * p4:2 * p4 + 2]),
                        deps=[t_exp[0], t_exp[1]], stream="dve")
                sig_rd = t_r
                for h01 in range(2):
                    h = 2 * p4 + h01
                    zslot = (p4 * 2 + h01) % 4
                    t_zs = E("vector",
                             lambda eng, sems, h=h, zslot=zslot:
                             eng.tensor_scalar_mul(zb[zslot][:], zb[zslot][:],
                                                   rr[:, h:h + 1]),
                             deps=[t_r], stream="dve")
                    t_zt = E("sync",
                             lambda eng, sems, zslot=zslot:
                             eng.dma_start_transpose(zT[zslot][:],
                                                     zb[zslot][:]),
                             deps=[t_zs, zT_war.get(zslot)], stream="qsy")
                    z_war[zslot] = t_zt
                ob = p4 % 2
                obase = 3072 + ob * 512
                for j in range(5):
                    for h01 in range(2):
                        h = 2 * p4 + h01
                        zslot = (p4 * 2 + h01) % 4
                        tpv2 = E("tensor",
                                 lambda eng, sems, j=j, h=h, h01=h01, qt=qt,
                                        zslot=zslot, obase=obase:
                                 eng.matmul(
                                     PS2[h01 * 64:(h01 + 1) * 64,
                                         obase:obase + 128],
                                     vF[:, (qt - 4 + j) % 6,
                                        h * 64:(h + 1) * 64],
                                     zT[zslot][:, j, :],
                                     start=(j == 0), stop=(j == 4),
                                     tile_position=(0, h01 * 64)),
                                 deps=[z_war[zslot], kv_ld[qt - 4 + j][1],
                                       o_war[ob] if j == 0 else None],
                                 stream="pe")
                        zT_war[zslot] = tpv2
                t_pv = sch.last_tok("pe")
                t_oe = E("vector",
                         lambda eng, sems, p4=p4, qi=qi, obase=obase:
                         eng.tensor_copy(
                             oT[:, p4, qi * 128:(qi + 1) * 128],
                             PS2[:, obase:obase + 128]),
                         deps=[t_pv], stream="dve")
                o_war[ob] = t_oe
            qT_war[qb] = sch.last_tok("pe")
            mk_war[qb] = sch.last_tok("dve")
            t_pelast = sch.last_tok("pe")
            for jj in range(5):
                ring_war[(qt - 4 + jj) % 6] = t_pelast
        t_ph2 = dict(pe=sch.last_tok("pe"), dve=sch.last_tok("dve"),
                     act=sch.last_tok("act"), qsy=sch.last_tok("qsy"))
        yield P2
        L_kv.close()

        phase_fence()
        P3 = ExitStack()
        wo_war = [None, None]
        for qi in range(nat):
            t = at0 + qi
            wb2 = qi % 2
            for ic in range(4):
                for sl in range(2):
                    E("tensor",
                      lambda eng, sems, ic=ic, sl=sl, qi=qi, wb2=wb2:
                      eng.matmul(
                          PS2[:, wb2 * 1024 + sl * 512:
                              wb2 * 1024 + (sl + 1) * 512],
                          oT[:, ic, qi * 128:(qi + 1) * 128],
                          fwo_sb[:, ic, sl * 512:(sl + 1) * 512],
                          start=(ic == 0), stop=(ic == 3)),
                      deps=[t_wo, wo_war[wb2]]
                      if (ic == 0 and sl == 0) else (), stream="pe")
            t_mm = sch.last_tok("pe")
            t_res = E("vector", lambda eng, sems, t=t, wb2=wb2:
                      eng.scalar_tensor_tensor(
                          xt(t), PS2[:, wb2 * 1024:(wb2 + 1) * 1024],
                          1.0, xt(t), ALU.mult, ALU.add),
                      deps=[t_mm], stream="dve")
            wo_war[wb2] = t_res
        t_ph3 = dict(pe=sch.last_tok("pe"), dve=sch.last_tok("dve"))
        yield P3
        L_o.close()
        L_wo.close()

        phase_fence()
        P4 = ExitStack()
        sb4 = P4.enter_context
        GN = 5
        GW = GN * 128
        hgr = sb4(nc.sbuf_tensor(f"hgr{layer}", [128, 8, GW], BF16))
        abuf = sb4(nc.sbuf_tensor(f"abuf{layer}", [128, 22, GW], BF16))
        gsc = sb4(nc.sbuf_tensor(f"gsc{layer}", [128, GW], BF16))
        w1s = [sb4(nc.sbuf_tensor(f"w1s{layer}_{i}", [128, 8, 128], BF16))
               for i in range(2)]
        w2f = sb4(nc.sbuf_tensor(f"w2f{layer}", [128, 22, 1024], BF16))
        scr4 = sb4(nc.sbuf_tensor(f"scr4{layer}", [1, 1536], F32))
        lgb = sb4(nc.sbuf_tensor(f"lgb{layer}", [128, 16], F32))
        b1c = sb4(nc.sbuf_tensor(f"b1c{layer}", [128, 44], F32))
        b2b = sb4(nc.sbuf_tensor(f"b2b{layer}", [128, 1024], F32))
        xcg = sb4(nc.sbuf_tensor(f"xcg{layer}", [128, 1024], BF16))
        nbuf = sb4(nc.sbuf_tensor(f"nbuf{layer}", [128, 512], F32))
        shf_sb = sb4(nc.sbuf_tensor(f"shf{layer}", [128, 4, 1], BF16))
        hTs4 = [sb4(nc.sbuf_tensor(f"hTs4{layer}_{i}", [128, 8, 128], BF16))
                for i in range(2)]

        t_lg = E("scalar", lambda eng, sems, layer=layer:
                 eng.dma_start(out=lgb[:, 0:8], in_=flg[layer]),
                 deps=[], stream="qsc")
        t_lb = E("scalar", lambda eng, sems, layer=layer:
                 eng.dma_start(out=lgb[:, 8:16], in_=flb[layer]),
                 deps=[], stream="qsc")
        t_b1 = E("scalar", lambda eng, sems, layer=layer:
                 eng.dma_start(out=b1c[:], in_=fb1[layer]),
                 deps=[], stream="qsc")
        t_b2r = E("scalar", lambda eng, sems, layer=layer:
                  eng.dma_start(out=scr4[0:1, 0:1024], in_=fb2row[layer]),
                  deps=[], stream="qsc")
        t_w2f = E("scalar", lambda eng, sems, layer=layer:
                  eng.dma_start(out=w2f[:],
                                in_=w2bd[layer].rearrange("(c p) o -> p c o",
                                                          p=128)),
                  deps=[wcvt_toks[("w2", layer)]], stream="qsc")
        t_b2m = []
        for hlf in range(2):
            t_b2m.append(E(
                "tensor", lambda eng, sems, hlf=hlf:
                eng.matmul(PS2[:, 2048 + hlf * 512:2048 + (hlf + 1) * 512],
                           ones1b_sb[:],
                           scr4[0:1, hlf * 512:(hlf + 1) * 512],
                           start=True, stop=True),
                deps=[t_b2r], stream="pe"))
        t_b2b = E("vector", lambda eng, sems:
                  eng.tensor_copy(b2b[:], PS2[:, 2048:3072]),
                  deps=t_b2m, stream="dve")

        groups = []
        g0 = 0
        while g0 < nmt:
            groups.append((mt0 + g0, min(GN, nmt - g0)))
            g0 += GN
        tp_war4 = None
        hts4_war = [None, None]
        hgr_rd = None
        xcg_war = None
        sum_war = t_b2b
        rb_rd = t_b2b
        w2ps_rd = None
        shf_wr = None
        first_group = True
        for (gt0, gn) in groups:
            gw = gn * 128
            segs = [(0, 512), (512, gw)] if gw > 512 else [(0, gw)]
            if not first_group:
                t_shc = E("vector", lambda eng, sems:
                          eng.tensor_copy(hgr[:, 4:8, 0:1],
                                          shf_sb[:, :, 0:1]),
                          deps=[shf_wr, hgr_rd], stream="dve")
            krange = range(-1, gn) if first_group else range(gn)
            for ki, k in enumerate(krange):
                t = gt0 + k
                hb4 = ki % 2
                t_c = E("vector", lambda eng, sems, t=t:
                        eng.tensor_copy(xcg[:], xt(t)),
                        deps=[xcg_war], stream="dve")
                t_tr4 = E("sync", lambda eng, sems, hb4=hb4:
                          eng.dma_start_transpose(hTs4[hb4][:], xcg[:]),
                          deps=[t_c, hts4_war[hb4]], stream="qsy")
                xcg_war = t_tr4
                col = k * 128
                if k >= 0:
                    E("vector", lambda eng, sems, col=col, hb4=hb4:
                      eng.tensor_copy(
                          hgr[:, 0:4, col:col + 128],
                          hTs4[hb4][:, 0:4, :]),
                      deps=[t_tr4, hgr_rd], stream="dve")
                    t_e2 = E("vector", lambda eng, sems, col=col, hb4=hb4:
                             eng.tensor_copy(
                                 hgr[:, 4:8, col + 1:col + 128],
                                 hTs4[hb4][:, 4:8, 0:127]),
                             deps=[t_tr4, hgr_rd], stream="dve")
                    if k < gn - 1:
                        t_e3 = E("vector", lambda eng, sems, col=col, hb4=hb4:
                                 eng.tensor_copy(
                                     hgr[:, 4:8, col + 128:col + 129],
                                     hTs4[hb4][:, 4:8, 127:128]),
                                 deps=[t_tr4], stream="dve")
                    else:
                        t_e3 = E("vector", lambda eng, sems, hb4=hb4:
                                 eng.tensor_copy(
                                     shf_sb[:, :, 0:1],
                                     hTs4[hb4][:, 4:8, 127:128]),
                                 deps=[t_tr4], stream="dve")
                        shf_wr = t_e3
                else:
                    t_e3 = E("vector", lambda eng, sems, hb4=hb4:
                             eng.tensor_copy(
                                 hgr[:, 4:8, 0:1],
                                 hTs4[hb4][:, 4:8, 127:128]),
                             deps=[t_tr4, hgr_rd], stream="dve")
                hts4_war[hb4] = t_e3
            first_group = False
            t_hgr = sch.last_tok("dve")
            for ic in range(8):
                for (a, b) in segs:
                    E("tensor", lambda eng, sems, ic=ic, a=a, b=b:
                      eng.matmul(PS2[0:1, 3072 + a:3072 + b], onescol_sb[:],
                                 hgr[:, ic, a:b], start=(ic == 0),
                                 stop=(ic == 7)),
                      deps=[t_hgr, sum_war]
                      if (ic == 0 and a == 0) else (), stream="pe")
            t_sum = sch.last_tok("pe")
            t_mu = E("vector", lambda eng, sems, gw=gw:
                     eng.tensor_scalar_mul(scr4[0:1, 0:gw],
                                           PS2[0:1, 3072:3072 + gw],
                                           1.0 / 1024.0),
                     deps=[t_sum, rb_rd], stream="dve")
            sq_last = t_mu
            for ic in range(8):
                t_s = E("scalar", lambda eng, sems, ic=ic, gw=gw:
                        eng.activation(gsc[:, 0:gw], hgr[:, ic, 0:gw],
                                       AF.Square),
                        deps=[t_hgr, sq_last], stream="act")
                for (a, b) in segs:
                    t_s2 = E("tensor", lambda eng, sems, ic=ic, a=a, b=b:
                             eng.matmul(PS2[0:1, 3072 + a:3072 + b],
                                        onescol_sb[:],
                                        gsc[:, a:b], start=(ic == 0),
                                        stop=(ic == 7)),
                             deps=[t_s] + ([t_mu] if ic == 0 and a == 0
                                           else []),
                             stream="pe")
                sq_last = t_s2
            t_v1 = E("vector", lambda eng, sems, gw=gw:
                     eng.tensor_scalar_mul(scr4[0:1, 768:768 + gw],
                                           PS2[0:1, 3072:3072 + gw],
                                           1.0 / 1024.0),
                     deps=[sq_last], stream="dve")
            t_v2 = E("vector", lambda eng, sems, gw=gw:
                     eng.tensor_mul(PS2[0:1, 3072:3072 + gw],
                                    scr4[0:1, 0:gw], scr4[0:1, 0:gw]),
                     deps=[t_v1], stream="dve")
            t_v3 = E("vector", lambda eng, sems, gw=gw:
                     eng.tensor_sub(scr4[0:1, 768:768 + gw],
                                    scr4[0:1, 768:768 + gw],
                                    PS2[0:1, 3072:3072 + gw]),
                     deps=[t_v2], stream="dve")
            sum_war = t_v3
            t_v4 = E("vector", lambda eng, sems, gw=gw:
                     eng.tensor_scalar(scr4[0:1, 768:768 + gw],
                                       scr4[0:1, 768:768 + gw], 1e-5, None,
                                       ALU.add),
                     deps=[t_v3], stream="dve")
            t_v5 = E("scalar", lambda eng, sems, gw=gw:
                     eng.activation(scr4[0:1, 768:768 + gw],
                                    scr4[0:1, 768:768 + gw], AF.Sqrt),
                     deps=[t_v4], stream="act")
            t_v6 = E("vector", lambda eng, sems, gw=gw:
                     eng.reciprocal(scr4[0:1, 768:768 + gw],
                                    scr4[0:1, 768:768 + gw]),
                     deps=[t_v5], stream="dve")
            t_v7 = E("vector", lambda eng, sems, gw=gw:
                     eng.tensor_mul(scr4[0:1, 0:gw], scr4[0:1, 0:gw],
                                    scr4[0:1, 768:768 + gw]),
                     deps=[t_v6], stream="dve")
            t_v8 = E("vector", lambda eng, sems, gw=gw:
                     eng.tensor_scalar_mul(scr4[0:1, 0:gw],
                                           scr4[0:1, 0:gw], -1.0),
                     deps=[t_v7], stream="dve")
            for (a, b) in segs:
                t_r1 = E("tensor", lambda eng, sems, a=a, b=b:
                         eng.matmul(PS2[:, 2048:2048 + (b - a)],
                                    ones1b_sb[:], scr4[0:1, 768 + a:768 + b],
                                    start=True, stop=True),
                         deps=[t_v8, rb_rd], stream="pe")
                t_r2 = E("tensor", lambda eng, sems, a=a, b=b:
                         eng.matmul(PS2[:, 2560:2560 + (b - a)],
                                    ones1b_sb[:], scr4[0:1, a:b],
                                    start=True, stop=True),
                         deps=[t_v8], stream="pe")
                for ic in range(8):
                    t_n1 = E("vector",
                             lambda eng, sems, ic=ic, a=a, b=b:
                             eng.tensor_mul(nbuf[:, 0:b - a],
                                            hgr[:, ic, a:b],
                                            PS2[:, 2048:2048 + (b - a)]),
                             deps=[t_r1, w.get("nbuf")], stream="dve")
                    t_n2 = E("vector",
                             lambda eng, sems, ic=ic, a=a, b=b:
                             eng.tensor_add(nbuf[:, 0:b - a],
                                            nbuf[:, 0:b - a],
                                            PS2[:, 2560:2560 + (b - a)]),
                             deps=[t_n1, t_r2], stream="dve")
                    t_n3 = E("vector",
                             lambda eng, sems, ic=ic, a=a, b=b:
                             eng.tensor_scalar(hgr[:, ic, a:b],
                                               nbuf[:, 0:b - a],
                                               lgb[:, ic:ic + 1],
                                               lgb[:, 8 + ic:9 + ic],
                                               ALU.mult, ALU.add),
                             deps=[t_n2, t_lg, t_lb], stream="dve")
                    w["nbuf"] = t_n3
                rb_rd = t_n3
            t_ln = sch.last_tok("dve")
            w1_rd = [None, None]
            w1ps_rd = None
            gsc_rd = None
            for oc in range(44):
                wbuf = oc % 2
                t_wl = E("scalar", lambda eng, sems, oc=oc, layer=layer,
                                          wbuf=wbuf:
                         eng.dma_start(
                             out=w1s[wbuf][:],
                             in_=w1bd[layer].rearrange("(c p) o -> p c o",
                                                       p=128)
                             [:, :, oc * 128:(oc + 1) * 128]),
                         deps=[w1_rd[wbuf], wcvt_toks[("w1", layer)]],
                         stream="qsc")
                for ic in range(8):
                    for (a, b) in segs:
                        E("tensor",
                          lambda eng, sems, ic=ic, a=a, b=b, wbuf=wbuf:
                          eng.matmul(PS2[:, a:b], w1s[wbuf][:, ic, :],
                                     hgr[:, ic, a:b],
                                     start=(ic == 0), stop=(ic == 7)),
                          deps=[t_wl, t_ln, w1ps_rd]
                          if (ic == 0 and a == 0) else (), stream="pe")
                t_w1 = sch.last_tok("pe")
                w1_rd[wbuf] = t_w1
                if oc < 22:
                    t_ev = E("scalar", lambda eng, sems, oc=oc, gw=gw:
                             eng.activation(abuf[:, oc, 0:gw],
                                            PS2[:, 0:gw], AF.Identity,
                                            bias=b1c[:, oc:oc + 1]),
                             deps=[t_w1, t_b1, w.get(("ab", oc))],
                             stream="act")
                    w1ps_rd = t_ev
                else:
                    t_ev = E("scalar", lambda eng, sems, oc=oc, gw=gw:
                             eng.activation(gsc[:, 0:gw], PS2[:, 0:gw],
                                            AF.Gelu, bias=b1c[:, oc:oc + 1]),
                             deps=[t_w1, t_b1, gsc_rd], stream="act")
                    t_gm = E("vector", lambda eng, sems, oc=oc, gw=gw:
                             eng.tensor_mul(abuf[:, oc - 22, 0:gw],
                                            abuf[:, oc - 22, 0:gw],
                                            gsc[:, 0:gw]),
                             deps=[t_ev], stream="dve")
                    gsc_rd = t_gm
                    w[("ab", oc - 22)] = t_gm
                    w1ps_rd = t_ev
            t_geglu = sch.last_tok("dve")
            for k in range(gn):
                t = gt0 + k
                for hc in range(22):
                    for sl2 in range(2):
                        E("tensor",
                          lambda eng, sems, hc=hc, sl2=sl2, k=k:
                          eng.matmul(
                              PS2[:, 3072 + sl2 * 512:3072 + (sl2 + 1) * 512],
                              abuf[:, hc, k * 128:(k + 1) * 128],
                              w2f[:, hc, sl2 * 512:(sl2 + 1) * 512],
                              start=(hc == 0), stop=(hc == 21)),
                          deps=[t_geglu, t_w2f, w2ps_rd, sum_war]
                          if (hc == 0 and sl2 == 0) else (), stream="pe")
                t_mm2 = sch.last_tok("pe")
                t_a1 = E("vector", lambda eng, sems, t=t:
                         eng.scalar_tensor_tensor(
                             xt(t), PS2[:, 3072:4096], 1.0, xt(t),
                             ALU.mult, ALU.add),
                         deps=[t_mm2, t_b2b, w.get("mres")], stream="dve")
                w2ps_rd = t_a1
                t_a2 = E("vector", lambda eng, sems, t=t:
                         eng.tensor_add(xt(t), xt(t), b2b[:]),
                         deps=[t_a1], stream="dve")
                w["mres"] = t_a2
            hgr_rd = sch.last_tok("pe")
        if layer == 0:
            t_rz = sch.last_tok("dve")
            for t in range(5, 10):
                t_rz = E("vector", lambda eng, sems, t=t:
                         eng.tensor_scalar(xt(t), xt(t), hk_sb[:, 0:1],
                                           None, ALU.mult),
                         deps=[t_rz], stream="dve")
        t_ph4 = dict(pe=sch.last_tok("pe"), dve=sch.last_tok("dve"),
                     act=sch.last_tok("act"), qgp=sch.last_tok("qgp"),
                     qsc=sch.last_tok("qsc"))
        layer_bar = [t_ph4["pe"], t_ph4["dve"], t_ph4["act"],
                     t_ph4["qgp"], t_ph4["qsc"], t_ph2["qsy"]]
        yield P4

    phase_fence()
    PF = ExitStack()
    sbF = PF.enter_context
    x1t = sbF(nc.sbuf_tensor("x1t", [128, 1024], F32))
    yt = sbF(nc.sbuf_tensor("yt", [128, 1024], F32))
    ot = sbF(nc.sbuf_tensor("ot", [128, 1024], F32))
    ldx = None
    sums_q = None
    sums_s = None
    st_war = None
    for ct in range(CHUNK // 128):
        t = HALO // 128 + ct
        t_x1 = E("gpsimd", lambda eng, sems, ct=ct:
                 eng.dma_start(out=x1t[:],
                               in_=x1d[ct * 128:(ct + 1) * 128, :]),
                 deps=[ldx], stream="qgp")
        t_y = E("vector", lambda eng, sems, t=t:
                eng.tensor_add(yt[:], xt(t), x1t[:]),
                deps=[t_x1, w.get("yt")], stream="dve")
        ldx = t_y
        t_s = E("vector", lambda eng, sems:
                eng.reduce_sum(sums_sb[:, 0:1], yt[:], axis=AX.X),
                deps=[t_y], stream="dve")
        t_q = E("scalar", lambda eng, sems:
                eng.activation(lnsc[:], yt[:], AF.Square,
                               accum_out=sums_sb[:, 1:2]),
                deps=[t_y, sums_q], stream="act")
        t_mu = E("vector", lambda eng, sems:
                 eng.tensor_scalar_mul(sums_sb[:, 2:3], sums_sb[:, 0:1],
                                       1.0 / 1024.0),
                 deps=[t_s], stream="dve")
        t_m2b = E("vector", lambda eng, sems:
                  eng.tensor_tensor(sums_sb[:, 3:4], sums_sb[:, 2:3],
                                    sums_sb[:, 2:3], ALU.mult),
                  deps=[t_mu], stream="dve")
        t_va = E("vector", lambda eng, sems:
                 eng.tensor_scalar(sums_sb[:, 4:5], sums_sb[:, 1:2],
                                   1.0 / 1024.0, None, ALU.mult),
                 deps=[t_q], stream="dve")
        t_vb = E("vector", lambda eng, sems:
                 eng.tensor_sub(sums_sb[:, 4:5], sums_sb[:, 4:5],
                                sums_sb[:, 3:4]),
                 deps=[t_va, t_m2b], stream="dve")
        t_vc = E("vector", lambda eng, sems:
                 eng.tensor_scalar(sums_sb[:, 5:6], sums_sb[:, 4:5],
                                   1e-5, None, ALU.add),
                 deps=[t_vb], stream="dve")
        t_sq2 = E("scalar", lambda eng, sems:
                  eng.activation(sums_sb[:, 6:7], sums_sb[:, 5:6], AF.Sqrt),
                  deps=[t_vc, sums_s], stream="act")
        t_rs = E("vector", lambda eng, sems:
                 eng.reciprocal(sums_sb[:, 7:8], sums_sb[:, 6:7]),
                 deps=[t_sq2], stream="dve")
        t_o = E("vector", lambda eng, sems:
                eng.tensor_scalar(ot[:], yt[:], sums_sb[:, 2:3],
                                  sums_sb[:, 7:8], ALU.subtract, ALU.mult),
                deps=[t_rs, st_war], stream="dve")
        w["yt"] = t_o
        sums_q = t_va
        sums_s = t_rs
        st_war = E("gpsimd", lambda eng, sems, ct=ct:
                   eng.dma_start(out=out_t[ct * 128:(ct + 1) * 128, :],
                                 in_=ot[:]),
                   deps=[t_o], stream="qgp")
    tq = sch.last_tok("qgp")
    E("gpsimd", lambda eng, sems: eng.nop(), deps=[tq], stream="gp")
    yield PF


from contextlib import ExitStack


def build_program(debug="full"):
    nc = bass.Bass("TRN2", target_bir_lowering=False, debug=False)
    sch = Sched()

    def din(name, shape, dt=F32):
        return nc.dram_tensor(name, list(shape), dt, kind="ExternalInput").ap()

    x0kvT = din("x0kvT", [1024, 8192])
    x0r = din("x0r", [RLEN, 1024])
    wkv_a = din("wkv_a", [1024, 1024])
    wq_a = din("wq_a", [1024, 512])
    bqkv_a = din("bqkv_a", [128, 12])
    ghost_a = din("ghost_a", [1, 8])
    wo_a = din("wo_a", [1024, 1024])
    amask_i = din("amask", [128, 4, 512])
    ident_f = din("ident_f32", [128, 128])
    ind8_i = din("ind8", [128, 4, 8], BF16)
    ind2_i = din("ind2f", [2, 128])
    ones1_i = din("ones1", [1, 128])
    par_i = din("parity", [1, 1], I32)
    myslot_i = din("myslot", [1, 1], I32)
    peslot_i = din("peslot", [1, 1], I32)
    magic_i = din("magic", [1, 8], I32)
    abase2_i = din("a_base2", [1, 1], I32)
    hk_i = din("halo_keep", [128, 1])
    bvrow_i = din("bvrow", [1, 512])
    borow_i = din("borow", [1, 1024])

    fwqkv_i = din("fwqkv", [2, 1024, 1536])
    fqs_i = din("fqs", [2, 1, 64])
    fks_i = din("fks", [2, 1, 64])
    fwo_i = din("fwo", [2, 512, 1024])
    flg_i = din("flg", [2, 128, 8])
    flb_i = din("flb", [2, 128, 8])
    fw1_i = din("fw1", [2, 1024, 2 * HIDP])
    fb1_i = din("fb1", [2, 128, 44])
    fw2_i = din("fw2", [2, HIDP, 1024])
    fb2row_i = din("fb2row", [2, 1, 1024])
    rot_cq_i = din("rot_cq", [RLEN, 64])
    rot_sq_i = din("rot_sq", [RLEN, 64])
    rot_ck_i = din("rot_ck", [RLEN, 64])
    rot_sk_i = din("rot_sk", [RLEN, 64])
    mask_l1_i = din("mask_l1", [22, 128, 640], BF16)
    mask_l2_i = din("mask_l2", [17, 128, 640], BF16)
    identb_i = din("ident_bf", [128, 128], BF16)
    onescol_i = din("onescol", [128, 1])

    QTd = nc.dram_tensor("QTd", [128, 4, ULEN], BF16, kind="Internal").ap()
    xchA = nc.dram_tensor("xchA", [2, 512, HALO + ULEN], BF16, kind="Internal",
                          addr_space="Shared").ap()
    AstD = nc.dram_tensor("AstD", [512, HALO + ULEN], BF16,
                          kind="Internal").ap()
    AtD = nc.dram_tensor("AtD", [1024, RLEN], BF16, kind="Internal").ap()
    xchF = nc.dram_tensor("xchF", [2, 8], I32, kind="Internal",
                          addr_space="Shared").ap()
    x1d = nc.dram_tensor("x1d", [CHUNK, 1024], F32, kind="Internal").ap()
    x1loD = nc.dram_tensor("x1loD", [512, 1024], F32, kind="Internal").ap()
    qtokD = nc.dram_tensor("qtokD", [26, 128, 512], BF16, kind="Internal").ap()
    kpD = nc.dram_tensor("kpD", [26, 128, 512], BF16, kind="Internal").ap()
    vFD = nc.dram_tensor("vFD", [26, 128, 512], BF16, kind="Internal").ap()
    w1bd = nc.dram_tensor("w1bd", [2, 1024, 2 * HIDP], BF16,
                          kind="Internal").ap()
    w2bd = nc.dram_tensor("w2bd", [2, HIDP, 1024], BF16, kind="Internal").ap()

    if debug == "x1":
        out_dbg = nc.dram_tensor("out_dbg", [RLEN, 1024], F32,
                                 kind="ExternalOutput").ap()
    out_t = nc.dram_tensor("out", [CHUNK, 1024], F32, kind="ExternalOutput").ap()

    for s, e, inc in (("qgp", "gpsimd", 16), ("qsy", "sync", 16),
                      ("qsc", "scalar", 16), ("wcv", "gpsimd", 16),
                      ("pe", "tensor", 1),
                      ("dve", "vector", 1), ("act", "scalar", 1),
                      ("gp", "gpsimd", 1), ("bridge", "gpsimd", 1),
                      ("xchg", "scalar", 1)):
        sch.stream(s, e, inc, nsem=4 if s in ("qgp", "pe", "dve", "act") else 2)
    for p in (0, 1):
        for s, e, inc in ((f"qgp{p}", "gpsimd", 16), (f"qsy{p}", "sync", 16),
                          (f"qsc{p}", "scalar", 16),
                          (f"pe{p}", "tensor", 1), (f"dve{p}", "vector", 1),
                          (f"act{p}", "scalar", 1)):
            sch.stream(s, e, inc, nsem=1)

    ES = ExitStack()
    blocks = []

    with ES:
        stA = ExitStack()
        sb = stA.enter_context
        wkv_sb = sb(nc.sbuf_tensor("wkv_sb", [128, 8, 1024], BF16))
        x0g = [sb(nc.sbuf_tensor(f"x0g{i}", [128, 8, 512], BF16)) for i in range(2)]
        KT = sb(nc.sbuf_tensor("KT", [128, 4, 8192], BF16))
        Vt = sb(nc.sbuf_tensor("Vt", [128, 64, 512], BF16))
        bq_sb = sb(nc.sbuf_tensor("bq_sb", [128, 12], F32))
        bq8_sb = sb(nc.sbuf_tensor("bq8_sb", [128, 4], F32))
        sq_sb = sb(nc.sbuf_tensor("sq_sb", [128, 512], BF16))
        sqd_sb = sb(nc.sbuf_tensor("sqd_sb", [128, 512], BF16))
        bvb_sb = sb(nc.sbuf_tensor("bvb_sb", [128, 512], F32))
        kn2m_sb = sb(nc.sbuf_tensor("kn2m_sb", [8, 16], F32))
        kpre_sb = sb(nc.sbuf_tensor("kpre_sb", [8, 16], F32))
        spre_sb = sb(nc.sbuf_tensor("spre_sb", [8, 16], F32))
        n2s_sb = sb(nc.sbuf_tensor("n2s_sb", [8, 512], F32))
        sds_sb = sb(nc.sbuf_tensor("sds_sb", [8, 512], F32))
        Mst_sb = sb(nc.sbuf_tensor("Mst_sb", [16, 512], BF16))
        MT_sb = sb(nc.sbuf_tensor("MT_sb", [128, 42, 16], BF16))
        MTf_sb = sb(nc.sbuf_tensor("MTf_sb", [128, 42, 16], F32))
        amask_sb = sb(nc.sbuf_tensor("amask_sb", [128, 4, 512], F32))
        identf_sb = sb(nc.sbuf_tensor("identf_sb", [128, 128], F32))
        ind8_sb = sb(nc.sbuf_tensor("ind8_sb", [128, 4, 8], BF16))
        ind2_sb = sb(nc.sbuf_tensor("ind2_sb", [2, 128], F32))
        ones1_sb = sb(nc.sbuf_tensor("ones1_sb", [1, 128], F32))
        gbc_sb = sb(nc.sbuf_tensor("gbc_sb", [128, 8], F32))
        par_sb = sb(nc.sbuf_tensor("par_sb", [1, 1], I32))
        myslot_sb = sb(nc.sbuf_tensor("myslot_sb", [1, 1], I32))
        peslot_sb = sb(nc.sbuf_tensor("peslot_sb", [1, 1], I32))
        magic_sb = sb(nc.sbuf_tensor("magic_sb", [1, 8], I32))
        fl_sb = sb(nc.sbuf_tensor("fl_sb", [1, 1], I32))
        QTt = [sb(nc.sbuf_tensor(f"QTt{i}", [128, 4, 128], BF16)) for i in range(2)]
        zA = [sb(nc.sbuf_tensor(f"zA{i}", [128, 1024], BF16)) for i in range(2)]
        zB = [sb(nc.sbuf_tensor(f"zB{i}", [128, 512], BF16)) for i in range(2)]
        zTA = [sb(nc.sbuf_tensor(f"zTA{i}", [128, 8, 128], BF16)) for i in range(2)]
        zTB = [sb(nc.sbuf_tensor(f"zTB{i}", [128, 4, 128], BF16)) for i in range(2)]
        sig_sb = sb(nc.sbuf_tensor("sig_sb", [128, 2, 12], F32))
        zmx_sb = sb(nc.sbuf_tensor("zmx_sb", [128, 2, 12], F32))
        sig2_sb = sb(nc.sbuf_tensor("sig2_sb", [128, 2], F32))
        zmx2_sb = sb(nc.sbuf_tensor("zmx2_sb", [128, 2], F32))
        den_sb = sb(nc.sbuf_tensor("den_sb", [128, 2], F32))
        r2_sb = sb(nc.sbuf_tensor("r2_sb", [128, 2], F32))
        rT_sb = sb(nc.sbuf_tensor("rT_sb", [2, 128], F32))
        rb_sb = sb(nc.sbuf_tensor("rb_sb", [128, 128], F32))
        Ast_sb = sb(nc.sbuf_tensor("Ast_sb", [128, 4, 128], BF16))

        QTst_sb = sb(nc.sbuf_tensor("QTst_sb", [128, 4, 512], BF16))

        PS = stA.enter_context(nc.psum_tensor("PS", [128, 4096], F32))

        def d(eng_name, **kw):
            return lambda eng, sems: None

        t_consts = []
        def ld(dst, src, stream="qgp", deps=()):
            return sch.emit(sch.streams[stream]["engine"],
                            lambda eng, sems, dst=dst, src=src:
                            eng.dma_start(out=dst, in_=src),
                            deps=deps, stream=stream)

        t_consts.append(ld(amask_sb[:], amask_i))
        t_consts.append(ld(identf_sb[:], ident_f))
        t_consts.append(ld(ind8_sb[:], ind8_i))
        t_consts.append(ld(ind2_sb[:], ind2_i))
        t_consts.append(ld(ones1_sb[:], ones1_i))
        t_consts.append(ld(bq_sb[:], bqkv_a))
        t_consts.append(ld(par_sb[:], par_i))
        t_consts.append(ld(myslot_sb[:], myslot_i))
        t_consts.append(ld(peslot_sb[:], peslot_i))
        t_consts.append(ld(magic_sb[:], magic_i))
        tg = ld(gbc_sb[0:1, :], ghost_a)
        t_consts.append(tg)
        tc = sch.last_tok("qgp")

        bvrow_sb = sb(nc.sbuf_tensor("bvrow_sb", [1, 512], F32))
        t_consts.append(ld(bvrow_sb[:], bvrow_i))
        tc = sch.last_tok("qgp")
        t_bq8 = sch.emit("vector", lambda eng, sems:
                         eng.tensor_scalar_mul(bq8_sb[:], bq_sb[:, 0:4], 0.125),
                         deps=[tc], stream="dve")
        t_gb = sch.emit("tensor", lambda eng, sems:
                        eng.matmul(PS[:, 3584:3592], ones1_sb[:],
                                   gbc_sb[0:1, :], start=True, stop=True),
                        deps=[tc], stream="pe")
        t_gb2 = sch.emit("vector", lambda eng, sems:
                         eng.tensor_copy(gbc_sb[:], PS[:, 3584:3592]),
                         deps=[t_gb], stream="dve")
        t_bv3 = sch.emit("tensor", lambda eng, sems:
                         eng.matmul(PS[:, 3584:4096], ones1_sb[:], bvrow_sb[:],
                                    start=True, stop=True),
                         deps=[tc, t_gb2], stream="pe")
        t_bv = sch.emit("vector", lambda eng, sems:
                        eng.tensor_copy(bvb_sb[:], PS[:, 3584:4096]),
                        deps=[t_bv3], stream="dve")

        zero_sb = sb(nc.sbuf_tensor("zero_sb", [128, 4, 128], BF16))
        t_z0 = sch.emit("vector", lambda eng, sems:
                        eng.memset(zero_sb[:], 0.0), deps=(), stream="dve")
        for zt in range(HALO // 128):
            sch.emit("scalar", lambda eng, sems, zt=zt:
                     eng.dma_start(
                         out=AstD.rearrange("(c p) n -> p c n", p=128)
                             [:, :, zt * 128:(zt + 1) * 128],
                         in_=zero_sb[:]),
                     deps=[t_z0], stream="qsc")
        t_zero = sch.last_tok("qsc")

        def idx_fn(eng, sems):
            eng.wait_ge(sems[tc.sem], tc.val)
            return eng.value_load(par_sb[:, :])

        sch.switch_begin(idx_fn, 2)
        emit_stageA_body(sch, 0, locals())
        sch.switch_case()
        emit_stageA_body(sch, 1, locals())
        sch.switch_end()
        t_bridge = sch.manual_advance("bridge", 1)

        def publish(eng, sems):
            my = eng.value_load(myslot_sb[:, :])
            return eng.dma_start(out=xchA[DS(my, 1)], in_=AstD[None, :, :])
        t_pub = sch.emit("scalar", publish, deps=[t_bridge], stream="qsc")

        def flag_write(eng, sems):
            my = eng.value_load(myslot_sb[:, :])
            return eng.dma_start(out=xchF[DS(my, 1), 0:1], in_=magic_sb[:, 0:1])
        t_fw = sch.emit("scalar", flag_write, deps=[t_pub], stream="qsc")

        def poll(eng, sems):
            pe_r = eng.value_load(peslot_sb[:, :])
            eng.memzero(fl_sb[:])
            with eng.While(lambda: eng.value_load(fl_sb[:, :]) != (MAGIC0 + 1)):
                eng.sem_clear(sems["poll.0"])
                eng.dma_start(out=fl_sb[:], in_=xchF[DS(pe_r, 1), 0:1]) \
                   .then_inc(sems["poll.0"], 16)
                eng.wait_ge(sems["poll.0"], 16)
            return None
        t_poll = sch.emit("scalar", poll, deps=[t_fw], stream="xchg")
        ab2_sb = sb(nc.sbuf_tensor("ab2_sb", [1, 1], I32))
        t_ab2 = ld(ab2_sb[:], abase2_i)

        b2_reg = [None]

        def ld_b2(eng, sems):
            b2_reg[0] = eng.value_load(ab2_sb[:, :])
            return None
        sch.emit("scalar", ld_b2, deps=[t_poll, t_ab2], stream=None)
        for sl in range(2):
            sch.emit("scalar", lambda eng, sems, sl=sl:
                     eng.dma_start(out=AtD[sl * 512:(sl + 1) * 512, :],
                                   in_=xchA[sl, :, DS(b2_reg[0], RLEN)]),
                     deps=(), stream="qsc")
        t_atd = sch.last_tok("qsc")

        sems = {}
        for name in sch.sem_names():
            sems[name] = ES.enter_context(nc.semaphore(name.replace(".", "_")))
        sems["poll.0"] = ES.enter_context(nc.semaphore("pollsem"))

        opsA = sch.cut()
        with nc.Block() as blk:
            for ename in ENGINES:
                if opsA[ename]:
                    getattr(blk, ename)(
                        lambda eng, ops_e=opsA[ename]:
                        Sched.materialize(ops_e, eng, sems))
        stA.close()

        stB = ExitStack()
        sb2 = stB.enter_context
        xres = sb2(nc.sbuf_tensor("xres", [128, RLEN // 128 - 4, 1024], F32))
        hk_sb = sb2(nc.sbuf_tensor("hk_sb", [128, 1], F32))
        lnsc = sb2(nc.sbuf_tensor("lnsc", [128, 1024], BF16))
        sums_sb = sb2(nc.sbuf_tensor("sums_sb", [128, 8], F32))
        ones1b_sb = sb2(nc.sbuf_tensor("ones1b", [1, 128], F32))
        identb_sb = sb2(nc.sbuf_tensor("identb_sb", [128, 128], BF16))
        onescol_sb = sb2(nc.sbuf_tensor("onescol_sb", [128, 1], BF16))
        wc_sb = [sb2(nc.sbuf_tensor(f"wc{i}", [128, 1408], BF16))
                 for i in range(2)]

        stW = ExitStack()
        sbW = stW.enter_context
        xstg = sbW(nc.sbuf_tensor("xstg", [128, 1024], F32))
        At_sb = sbW(nc.sbuf_tensor("At_sb", [128, 8, 128], BF16))
        wo_sb = sbW(nc.sbuf_tensor("wo_sb", [128, 8, 1024], BF16))
        bob_sb = sbW(nc.sbuf_tensor("bob_sb", [128, 1024], F32))
        x0t_sb = [sbW(nc.sbuf_tensor(f"x0t{i}", [128, 1024], F32)) for i in range(2)]
        xpre_sb = sbW(nc.sbuf_tensor("xpre_sb", [128, 1024], F32))
        borow_sb = sbW(nc.sbuf_tensor("borow_sb", [1, 1024], F32))

        PS2 = stB.enter_context(nc.psum_tensor("PS2", [128, 4096], F32))

        t2c = []
        t2c.append(ld(wo_sb[:],
                      wo_a.rearrange("(c p) o -> p c o", p=128)))
        t2c.append(ld(hk_sb[:], hk_i))
        t2c.append(ld(ones1b_sb[:], ones1_i))
        t2c.append(ld(borow_sb[:], borow_i))
        t2c.append(ld(identb_sb[:], identb_i))
        t2c.append(ld(onescol_sb[:], onescol_i))
        tc2 = sch.last_tok("qgp")
        t_bo = []
        for half in range(2):
            tt = sch.emit("tensor", lambda eng, sems, h=half:
                          eng.matmul(PS2[:, 2048 + h * 512:2048 + (h + 1) * 512],
                                     ones1b_sb[:],
                                     borow_sb[:, h * 512:(h + 1) * 512],
                                     start=True, stop=True),
                          deps=[tc2], stream="pe")
            t_bo.append(tt)
        t_bob = sch.emit("vector", lambda eng, sems:
                         eng.tensor_copy(bob_sb[:], PS2[:, 2048:3072]),
                         deps=t_bo, stream="dve")

        emit_wo_x1(sch, locals())

        t_entry = [sch.last_tok(s)
                   for s in ("pe", "dve", "act", "qgp", "qsc", "qsy")]
        stW.close()

        def mat_block(ops):
            with nc.Block() as blk:
                for ename in ENGINES:
                    if ops[ename]:
                        getattr(blk, ename)(
                            lambda eng, ops_e=ops[ename]:
                            Sched.materialize(ops_e, eng, sems))

        if debug == "x1":
            for t in range(RLEN // 128):
                src = (x1loD[t * 128:(t + 1) * 128, :] if t < 4
                       else xres[:, t - 4, :])
                sch.emit("gpsimd",
                         lambda eng, sems, t=t, src=src:
                         eng.dma_start(out=out_dbg[t * 128:(t + 1) * 128, :],
                                       in_=src),
                         deps=[sch.last_tok("dve"), sch.last_tok("qgp")],
                         stream="qgp")
            tq = sch.last_tok("qgp")
            sch.emit("gpsimd", lambda eng, sems: None, deps=[tq], stream="gp")
            sch.emit("gpsimd",
                     lambda eng, sems:
                     eng.dma_start(out=out_t[0:128, :], in_=xres[:, 6, :]),
                     deps=[sch.last_tok("qgp")], stream="qgp")
            tq = sch.last_tok("qgp")
            sch.emit("gpsimd", lambda eng, sems: None, deps=[tq], stream="gp")
            mat_block(sch.cut())
        else:
            H = dict(locals())
            for scope in emit_stageB(sch, H):
                mat_block(sch.cut())
                scope.close()
        stB.close()
    return nc


def _declared_inputs(nc):
    names = []
    for mls in nc.m.functions[0].allocations:
        if getattr(mls, "kind", None) == "ExternalInput":
            n = mls.name
            if n.endswith("_set"):
                n = n[:-4]
            if n in ("partition_id", "dbg_addr"):
                continue
            names.append(n)
    return names


_NC_CACHE = {}


def kernel(**inputs):
    in_maps = build_host_inputs(inputs)
    if "full" not in _NC_CACHE:
        _NC_CACHE["full"] = build_program(debug="full")
    nc = _NC_CACHE["full"]
    decl = set(_declared_inputs(nc))
    im = []
    for m in in_maps:
        missing = decl - set(m)
        assert not missing, f"missing inputs: {missing}"
        im.append({k: v for k, v in m.items() if k in decl})
    res = run_bass_kernel_spmd(nc, im, core_ids=list(range(8)))
    out = np.zeros((2, 8192, 1024), np.float32)
    for c in range(8):
        cfg = core_cfg(c)
        out[cfg["batch"], cfg["s0"]:cfg["s1"]] = res.results[c]["out"]
    return out
